# version 09
# baseline (speedup 1.0000x reference)
import contextlib

import numpy as np
import ml_dtypes

import concourse.bass as bass
import concourse.mybir as mybir
from concourse.bass_utils import run_bass_kernel_spmd

F32 = mybir.dt.float32
BF16 = mybir.dt.bfloat16
I32 = mybir.dt.int32
U32 = mybir.dt.uint32
AF = mybir.ActivationFunctionType
ALU = mybir.AluOpType
AX = mybir.AxisListType
BF = ml_dtypes.bfloat16

D = 1024
NHEAD, HD, DROPE, LATD = 16, 64, 16, 256
FF0, NE = 4096, 8
FF1 = 3072
LNG = float(np.log(np.float32(0.9)))
EPS = 1e-5
QSC = 1.0 / 8.0
DEQ = float(np.float32(1.0) / np.float32(127.0))
NC = 8


def host_consts(T):
    f32 = np.float32
    j = np.arange(512, dtype=f32)
    scale_vec = (2 * j + f32(0.4 * 1024)) / f32(1.4 * 1024)
    power = np.arange(T, dtype=f32) / f32(512.0)
    scale = scale_vec[None, :] ** power[:, None]
    inv_freq = f32(1.0) / (f32(10000.0) ** (j / f32(512.0)))
    pos = np.arange(T, dtype=f32)[:, None]
    sinu = pos * inv_freq[None, :]
    sn, cs = np.sin(sinu).astype(f32), np.cos(sinu).astype(f32)

    def rep2(a):
        return np.ascontiguousarray(np.repeat(a.astype(f32), 2, axis=1))

    sgn = np.tile(np.asarray([-1.0, 1.0], f32), 512)[None, :]
    con = {
        "xq_cos": rep2(cs * scale),
        "xq_sin": np.ascontiguousarray(rep2(sn * scale) * sgn),
        "xk_cos": rep2(cs / scale),
        "xk_sin": np.ascontiguousarray(rep2(sn / scale) * sgn),
    }
    C = np.ones(D, f32)
    S = np.zeros(D, f32)
    inv2 = f32(1.0) / (f32(10000.0) ** (np.arange(0, DROPE, 2, dtype=f32) / f32(HD)))
    for h in range(NHEAD):
        for jj in range(DROPE // 2):
            a = f32(h) * inv2[jj]
            C[h * HD + 2 * jj] = np.cos(a)
            C[h * HD + 2 * jj + 1] = np.cos(a)
            S[h * HD + 2 * jj] = -np.sin(a)
            S[h * HD + 2 * jj + 1] = np.sin(a)
    con["ropeC"] = np.ascontiguousarray(np.broadcast_to(C, (128, D))).astype(f32)
    con["ropeS"] = np.ascontiguousarray(np.broadcast_to(S, (128, D))).astype(f32)
    con["ident_bf"] = np.eye(128).astype(BF)
    con["ident_f32"] = np.eye(128, dtype=f32)
    con["ones_f32"] = np.ones((1, 128), f32)
    con["ones_bf"] = np.ones((1, 512), BF)
    con["headmask2"] = np.stack(
        [(np.arange(128) < 64), (np.arange(128) >= 64)], axis=1).astype(f32)
    con["ltri"] = np.triu(np.ones((128, 128)), 1).astype(BF)
    return con


CONST_DTYPES = {
    "xq_cos": F32, "xq_sin": F32, "xk_cos": F32, "xk_sin": F32,
    "ropeC": F32, "ropeS": F32, "ident_bf": BF16, "ident_f32": F32,
    "ones_f32": F32, "ones_bf": BF16, "headmask2": F32, "ltri": BF16,
}


class Prog:
    ENGS = ("sync", "gpsimd", "tensor", "vector", "scalar")
    CSEM = {"tensor": "pe", "vector": "ve", "scalar": "se", "gpsimd": "gc"}

    def __init__(self, nc, block, stack):
        self.nc, self.block, self.stack = nc, block, stack
        self.sems = {}
        self.cnt = {}
        for k in ("pe", "ve", "se", "gc"):
            self._new_sem(k)
        self.pending = {e: [] for e in self.ENGS}
        self.waited = {e: {} for e in self.ENGS}
        self.bar = {e: [] for e in self.ENGS}

    def _new_sem(self, name):
        self.sems[name] = self.stack.enter_context(self.nc.semaphore(name))
        self.cnt[name] = 0

    def _push(self, eng, fn, sem, inc, waits):
        w = self.bar[eng] + [x for x in waits if x is not None and x[1] > 0]
        for s, v in w:
            assert v <= self.cnt[s], \
                f"wait on future token {s}={v} (cnt {self.cnt[s]})"
        self.bar[eng] = []
        self.cnt[sem] += inc
        tok = self.cnt[sem]
        self.pending[eng].append((w, fn, sem, inc))
        return tok

    def dma(self, eng, slot, out, in_, waits=(), **kw):
        if slot not in self.sems:
            self._new_sem(slot)
        tok = self._push(eng, lambda e: e.dma_start(out=out, in_=in_, **kw),
                         slot, 16, waits)
        return (slot, tok)

    def idma(self, slot, out, out_offset, in_, in_offset, waits=(), **kw):
        if slot not in self.sems:
            self._new_sem(slot)
        tok = self._push(
            "gpsimd",
            lambda e: e.indirect_dma_start(out=out, out_offset=out_offset,
                                           in_=in_, in_offset=in_offset, **kw),
            slot, 16, waits)
        return (slot, tok)

    def op(self, eng, meth, *args, waits=(), chain=True, **kw):
        sem = self.CSEM[eng]
        w = list(waits)
        if chain and eng != "tensor" and self.cnt[sem] > 0:
            w.append((sem, self.cnt[sem]))
        return self._push(eng, lambda e: getattr(e, meth)(*args, **kw), sem, 1, w)

    def mm(self, fn, waits=()):
        return self._push("tensor", fn, "pe", 1, waits)

    def tw(self, te, s, v):
        eng = self._cur
        if v > self.waited[eng].get(s, 0):
            te.wait_ge(self.sems[s], v)
            self.waited[eng][s] = v

    def flush(self):
        for eng in self.ENGS:
            steps = self.pending[eng]
            if not steps:
                continue
            self.pending[eng] = []

            def body(e, steps=steps, eng=eng):
                self._cur = eng
                for w, fn, sem, inc in steps:
                    for s, v in w:
                        if v > self.waited[eng].get(s, 0):
                            e.wait_ge(self.sems[s], v)
                            self.waited[eng][s] = v
                    fn(e).then_inc(self.sems[sem], inc)

            getattr(self.block, eng)(body)

    def barrier(self):
        self.flush()
        snap = [(s, v) for s, v in self.cnt.items() if v > 0]
        for e in self.ENGS:
            self.bar[e] = list(snap)

    def finish(self):
        self.barrier()
        snap = [(s, v) for s, v in self.cnt.items() if v > 0]

        def body(e):
            for s, v in snap:
                if v > self.waited["sync"].get(s, 0):
                    e.wait_ge(self.sems[s], v)
                    self.waited["sync"][s] = v

        self.block.sync(body)


_UNIQ = [0]


def consts_sb(pr, nc, st, aps, con, need):
    loaded = False
    for name in need:
        key = name + "_sb"
        if key in con:
            continue
        src = aps[name]
        _UNIQ[0] += 1
        h = st.enter_context(
            nc.sbuf_tensor(f"{key}_{_UNIQ[0]}", list(src.shape),
                           CONST_DTYPES[name]))
        pr.dma("sync", "cst_" + name, h[:], src)
        con[key] = h
        loaded = True
    if loaded:
        pr.barrier()


def emit_ln_pass(pr, st, nc, T, x_dram, xlnT, con, tag, router=None,
                 xlnT_dram=None, xln_tm_dram=None):
    nqt = T // 128
    E = st.enter_context
    xt = [E(nc.sbuf_tensor(f"{tag}_xt{i}", [128, D], F32)) for i in range(2)]
    bns = E(nc.sbuf_tensor(f"{tag}_bns", [128, 12], F32))
    mv = E(nc.sbuf_tensor(f"{tag}_mv", [128, 2], F32))
    rstd = E(nc.sbuf_tensor(f"{tag}_rstd", [128, 1], F32))
    nmr = E(nc.sbuf_tensor(f"{tag}_nmr", [128, 1], F32))
    tmp1 = E(nc.sbuf_tensor(f"{tag}_tmp1", [128, 1], F32))
    xln = [E(nc.sbuf_tensor(f"{tag}_xln{i}", [128, D], BF16)) for i in range(2)]
    pst = [E(nc.psum_tensor(f"{tag}_pst{i}", [128, D], BF16)) for i in range(2)]
    epsc = E(nc.sbuf_tensor(f"{tag}_eps", [128, 1], F32))
    pr.op("vector", "memset", epsc[:], EPS)
    ident = con["ident_bf_sb"]
    xlnT3 = xlnT[:].rearrange("p (c t) -> p c t", c=8)

    tok_xt_rd = [0, 0]
    tok_ev = [0, 0]
    tok_tp = [0, 0]
    tok_tm = [None, None]
    r = router
    for t in range(nqt):
        b = t % 2
        tok_x = pr.dma("sync", f"{tag}x{b}", xt[b][:],
                       x_dram[t * 128:(t + 1) * 128, :],
                       waits=[("ve", tok_xt_rd[b])])
        pr.op("vector", "bn_stats", bns[:, 0:6], xt[b][:, 0:512],
              waits=[tok_x])
        pr.op("vector", "bn_stats", bns[:, 6:12], xt[b][:, 512:1024])
        pr.op("vector", "bn_aggr", mv[:], bns[:])
        tok_mv = pr.cnt["ve"]
        pr.op("scalar", "activation", tmp1[:], mv[:, 1:2], AF.Ln,
              bias=epsc[:, 0:1], waits=[("ve", tok_mv)])
        tok_rstd = pr.op("scalar", "activation", rstd[:], tmp1[:], AF.Exp,
                         scale=-0.5)
        pr.op("vector", "tensor_scalar", nmr[:], mv[:, 0:1], rstd[:, 0:1],
              -1.0, op0=ALU.mult, op1=ALU.mult, waits=[("se", tok_rstd)])
        tok_xln = pr.op("vector", "tensor_scalar", xln[b][:], xt[b][:],
                        rstd[:, 0:1], nmr[:, 0:1], op0=ALU.mult, op1=ALU.add,
                        waits=[("pe", tok_tp[b]), tok_tm[b]])
        tok_xt_rd[b] = tok_xln
        if xln_tm_dram is not None:
            tok_tm[b] = pr.dma("sync", f"{tag}tm{b}",
                               xln_tm_dram[t * 128:(t + 1) * 128, :],
                               xln[b][:], waits=[("ve", tok_xln)])

        def tp(te, b=b, tok_xln=tok_xln, tev=tok_ev[b]):
            pr.tw(te, "ve", tok_xln)
            if tev:
                pr.tw(te, "ve", tev)
            last = None
            for c in range(8):
                last = te.transpose(pst[b][:, c * 128:(c + 1) * 128],
                                    xln[b][:, c * 128:(c + 1) * 128], ident[:])
            return last

        tok_tp[b] = pr._push("tensor", tp, "pe", 1, [])
        dst = xlnT3[:, :, t * 128:(t + 1) * 128]
        tok_ev[b] = pr.op("vector", "tensor_copy", dst,
                          pst[b][:].rearrange("p (c t) -> p c t", c=8),
                          waits=[("pe", tok_tp[b])])
        if xlnT_dram is not None:
            ddst = xlnT_dram.rearrange("(c p) t -> p c t", p=128)
            pr.dma("sync", f"{tag}st{b}", ddst[:, :, t * 128:(t + 1) * 128],
                   dst, waits=[("ve", tok_ev[b])])
        if r is not None:
            pp = r["psr"][t % 2]

            def rmm(te, t=t, b=b, pp=pp, tev=tok_ev[b],
                    tg=r["tok_gate"][t % 2], twr=r["tok_wr"],
                    tbr=r["tok_br"]):
                pr.tw(te, twr[0], twr[1])
                pr.tw(te, tbr[0], tbr[1])
                pr.tw(te, "ve", tev)
                if tg:
                    pr.tw(te, "ve", tg)
                for c in range(8):
                    te.matmul(pp[:],
                              lhsT=xlnT3[:, c, t * 128:(t + 1) * 128],
                              rhs=r["wr_sb"][:].rearrange(
                                  "p (c e) -> p c e", c=8)[:, c, :],
                              start=(c == 0), stop=False)
                return te.matmul(pp[:], lhsT=con["ones_bf_sb"][:, 0:128],
                                 rhs=r["br_sb"][:], start=False, stop=True)

            tok_rm = pr._push("tensor", rmm, "pe", 1, [])
            pr.op("vector", "tensor_reduce", r["negmx"][:], pp[:], axis=AX.X,
                  op=ALU.max, negate=True, waits=[("pe", tok_rm)])
            pr.op("vector", "tensor_scalar", r["mx"][:], r["negmx"][:], -1.0,
                  None, op0=ALU.mult)
            tok_e = pr.op("scalar", "activation", r["er"][:], pp[:], AF.Exp,
                          bias=r["negmx"][:, 0:1], accum_out=r["z8"][:],
                          waits=[("ve", pr.cnt["ve"])])
            pr.op("vector", "reciprocal", r["g8"][:], r["z8"][:],
                  waits=[("se", tok_e)])
            pr.op("vector", "tensor_scalar", r["sel"][:], pp[:],
                  r["mx"][:, 0:1], None, op0=ALU.is_equal)
            pr.op("vector", "scalar_tensor_tensor", r["mi"][:], r["sel"][:],
                  -1000.0, r["e_bc"][:], op0=ALU.mult, op1=ALU.add,
                  waits=[("gc", r["tok_ebc"])])
            pr.op("vector", "tensor_reduce", r["idxm"][:], r["mi"][:],
                  axis=AX.X, op=ALU.min)
            pr.op("vector", "tensor_scalar", r["idxp"][:], r["idxm"][:],
                  1000.0, None, op0=ALU.add)
            pr.op("vector", "tensor_scalar", r["self_"][:], r["e_bc"][:],
                  r["idxp"][:, 0:1], None, op0=ALU.is_equal)
            pr.op("vector", "tensor_copy",
                  r["sel4"][:, :, t:t + 1],
                  r["self_"][:].rearrange("p (e o) -> p e o", o=1))
            r["tok_gate"][t % 2] = pr.op("vector", "tensor_copy",
                                         r["wtok"][:, t:t + 1], r["g8"][:])
    pr.barrier()


def emit_phase_A(pr, nc, T, aps, con):
    nqt = T // 128
    QB = min(512, T)
    nqb = T // QB
    with contextlib.ExitStack() as stP:
        EP = stP.enter_context
        kfm = EP(nc.sbuf_tensor("A_kfm", [128, 8 * T], BF16))
        consts_sb(pr, nc, stP, aps, con,
                  need=("ident_bf", "ident_f32", "ones_f32", "ones_bf"))
        kfm3 = kfm[:].rearrange("p (c t) -> p c t", c=8)
        qfm_r = aps["qfm_d"].rearrange("(c p) t -> p c t", p=128)

        with contextlib.ExitStack() as st:
            E = st.enter_context
            xlnT = E(nc.sbuf_tensor("A_xlnT", [128, 8 * T], BF16))
            with contextlib.ExitStack() as stln:
                emit_ln_pass(pr, stln, nc, T, aps["x_in"], xlnT, con, "Aln")
            xlnT3 = xlnT[:].rearrange("p (c t) -> p c t", c=8)

            wb = E(nc.sbuf_tensor("A_wb", [128, 8 * D], BF16))
            psq = [E(nc.psum_tensor(f"A_psq{i}", [128, D], F32)) for i in range(2)]
            pstp = [E(nc.psum_tensor(f"A_pstp{i}", [128, D], BF16)) for i in range(2)]
            vbuf = [E(nc.sbuf_tensor(f"A_vb{i}", [128, D], BF16)) for i in range(2)]
            cosb = [E(nc.sbuf_tensor(f"A_cos{i}", [128, D], F32)) for i in range(2)]
            sinb = [E(nc.sbuf_tensor(f"A_sin{i}", [128, D], F32)) for i in range(2)]
            t1b = E(nc.sbuf_tensor("A_t1", [128, D], F32))
            swb = E(nc.sbuf_tensor("A_sw", [128, D], F32))
            qxb = [E(nc.sbuf_tensor(f"A_qx{i}", [128, D], BF16)) for i in range(2)]
            qfb = [E(nc.sbuf_tensor(f"A_qf{i}", [128, D], BF16)) for i in range(2)]
            wb3 = wb[:].rearrange("p (c o) -> p c o", c=8)

            for wname in ("WV", "WK", "WQ"):
                tok_w = pr.dma("gpsimd", "Awb", wb3,
                               aps[wname].rearrange("(c p) o -> p c o", p=128),
                               waits=[("pe", pr.cnt["pe"])])
                tok_rd_ve = [0, 0]
                tok_rd_se = [0, 0]
                tok_st = [None, None]
                tok_tp = [0, 0]
                tok_ev2 = [0, 0]
                pend_tp = None

                def flush_tp(pend, wname=wname):
                    t, b, tok_qx = pend

                    def tpf(te, b=b, tok_qx=tok_qx, tev=tok_ev2[b]):
                        pr.tw(te, "ve", tok_qx)
                        if tev:
                            pr.tw(te, "ve", tev)
                        last = None
                        for c in range(8):
                            last = te.transpose(
                                pstp[b][:, c * 128:(c + 1) * 128],
                                qxb[b][:, c * 128:(c + 1) * 128],
                                con["ident_bf_sb"][:])
                        return last

                    tok_tp[b] = pr._push("tensor", tpf, "pe", 1, [])
                    ps3 = pstp[b][:].rearrange("p (c t) -> p c t", c=8)
                    if wname == "WK":
                        tok_ev2[b] = pr.op(
                            "vector", "tensor_copy",
                            kfm3[:, :, t * 128:(t + 1) * 128], ps3,
                            waits=[("pe", tok_tp[b])])
                    else:
                        qf3 = qfb[b][:].rearrange("p (c t) -> p c t", c=8)
                        tok_ev2[b] = pr.op("vector", "tensor_copy", qf3, ps3,
                                           waits=[("pe", tok_tp[b]),
                                                  tok_st[b]])
                        tok_st[b] = pr.dma(
                            "sync", f"Aqst{b}",
                            qfm_r[:, :, t * 128:(t + 1) * 128], qf3,
                            waits=[("ve", tok_ev2[b])])

                for t in range(nqt):
                    b = t % 2

                    def qmm(te, t=t, b=b, tok_w=tok_w, trv=tok_rd_ve[b],
                            trs=tok_rd_se[b]):
                        pr.tw(te, tok_w[0], tok_w[1])
                        if trv:
                            pr.tw(te, "ve", trv)
                        if trs:
                            pr.tw(te, "se", trs)
                        last = None
                        for nh in range(2):
                            for c in range(8):
                                last = te.matmul(
                                    psq[b][:, nh * 512:(nh + 1) * 512],
                                    lhsT=xlnT3[:, c, t * 128:(t + 1) * 128],
                                    rhs=wb3[:, c, nh * 512:(nh + 1) * 512],
                                    start=(c == 0), stop=(c == 7))
                        return last

                    tok_mm = pr._push("tensor", qmm, "pe", 1, [])
                    if wname == "WV":
                        tok_ev = pr.op("vector", "tensor_copy", vbuf[b][:],
                                       psq[b][:],
                                       waits=[("pe", tok_mm), tok_st[b]])
                        tok_rd_ve[b] = tok_ev
                        tok_st[b] = pr.dma(
                            "sync", f"Avst{b}",
                            aps["vtm_d"][t * 128:(t + 1) * 128, :],
                            vbuf[b][:], waits=[("ve", tok_ev)])
                    else:
                        cn, sn = (("xk_cos", "xk_sin") if wname == "WK"
                                  else ("xq_cos", "xq_sin"))
                        tok_c = pr.dma("sync", f"Acos{b}", cosb[b][:],
                                       aps[cn][t * 128:(t + 1) * 128, :],
                                       waits=[("ve", tok_rd_ve[b])])
                        tok_s = pr.dma("sync", f"Asin{b}", sinb[b][:],
                                       aps[sn][t * 128:(t + 1) * 128, :],
                                       waits=[("ve", tok_rd_ve[b])])
                        pr.op("vector", "tensor_tensor", t1b[:], psq[b][:],
                              cosb[b][:], op=ALU.mult,
                              waits=[("pe", tok_mm), tok_c])
                        vps = psq[b][:].rearrange("p (n two) -> p n two", two=2)
                        vsw = swb[:].rearrange("p (n two) -> p n two", two=2)
                        pr.op("scalar", "copy", vsw[:, :, 0:1], vps[:, :, 1:2],
                              waits=[("pe", tok_mm), ("ve", pr.cnt["ve"])])
                        tok_sw = pr.op("scalar", "copy", vsw[:, :, 1:2],
                                       vps[:, :, 0:1])
                        tok_rd_se[b] = tok_sw
                        pr.op("vector", "tensor_tensor", swb[:], swb[:],
                              sinb[b][:], op=ALU.mult,
                              waits=[("se", tok_sw), tok_s])
                        tok_qx = pr.op("vector", "tensor_tensor", qxb[b][:],
                                       t1b[:], swb[:], op=ALU.add,
                                       waits=[("pe", tok_tp[b])])
                        tok_rd_ve[b] = tok_qx
                        if pend_tp is not None:
                            flush_tp(pend_tp)
                        pend_tp = (t, b, tok_qx)
                if pend_tp is not None:
                    flush_tp(pend_tp)
                    pend_tp = None
                pr.barrier()

        negBc = stP.enter_context(nc.sbuf_tensor("A_negBc", [128, nqt], F32))
        with contextlib.ExitStack() as st:
            E = st.enter_context
            qcol = [E(nc.sbuf_tensor(f"A_qc{i}", [128, 8 * 128], BF16))
                    for i in range(2)]
            psA = E(nc.psum_tensor("A_psA", [128, T], F32))
            esc = E(nc.sbuf_tensor("A_esc", [128, T], F32))
            negmx = E(nc.sbuf_tensor("A_negmx", [128, 1], F32))
            zt = E(nc.sbuf_tensor("A_zt", [128, 1], F32))
            lnz = E(nc.sbuf_tensor("A_lnz", [128, 1], F32))
            tok_exp = 0
            for t in range(nqt):
                b = t % 2
                tok_q = pr.dma("sync", f"Aqc{b}", qcol[b][:],
                               qfm_r[:, :, t * 128:(t + 1) * 128],
                               waits=[("pe", pr.cnt["pe"])])
                qc3 = qcol[b][:].rearrange("p (c t) -> p c t", c=8)

                def smm(te, tq=tok_q, qc3=qc3, texp=tok_exp, tve=pr.cnt["ve"]):
                    pr.tw(te, tq[0], tq[1])
                    if texp:
                        pr.tw(te, "se", texp)
                    if tve:
                        pr.tw(te, "ve", tve)
                    last = None
                    for kb in range(T // 512):
                        for c in range(8):
                            last = te.matmul(
                                psA[:, kb * 512:(kb + 1) * 512],
                                lhsT=qc3[:, c, :],
                                rhs=kfm3[:, c, kb * 512:(kb + 1) * 512],
                                start=(c == 0), stop=(c == 7))
                    return last

                tok_mm = pr._push("tensor", smm, "pe", 1, [])
                pr.op("vector", "tensor_reduce", negmx[:], psA[:], axis=AX.X,
                      op=ALU.max, negate=True, waits=[("pe", tok_mm)])
                tok_exp = pr.op("scalar", "activation", esc[:], psA[:], AF.Exp,
                                bias=negmx[:, 0:1], accum_out=zt[:],
                                waits=[("ve", pr.cnt["ve"])])
                pr.op("scalar", "activation", lnz[:], zt[:], AF.Ln)
                pr.op("vector", "tensor_tensor", negBc[:, t:t + 1], negmx[:],
                      lnz[:], op=ALU.subtract, waits=[("se", pr.cnt["se"])])
            pr.barrier()
        with contextlib.ExitStack() as st:
            E = st.enter_context
            ps32 = E(nc.psum_tensor("A_ps32", [nqt, 128], F32))
            nb32 = E(nc.sbuf_tensor("A_nb32", [nqt, 128], F32))
            tok_tp = pr.mm(lambda te: te.transpose(ps32[:], negBc[:],
                                                   con["ident_f32_sb"][:]))
            tok_cp = pr.op("vector", "tensor_copy", nb32[:], ps32[:],
                           waits=[("pe", tok_tp)])
            pr.dma("sync", "Abrow",
                   aps["brow_d"][0, :].rearrange("(a b) -> a b", a=nqt),
                   nb32[:], waits=[("ve", tok_cp)])
            pr.barrier()

        with contextlib.ExitStack() as st:
            E = st.enter_context
            nkt = T // 128
            vtm = E(nc.sbuf_tensor("A_vtm", [128, nkt * D], BF16))
            pD = E(nc.sbuf_tensor("A_pD", [128, nkt * QB], BF16))
            qqb = E(nc.sbuf_tensor("A_qqb", [128, 8 * QB], BF16))
            mt = E(nc.sbuf_tensor("A_mt", [128, QB], F32))
            dm = E(nc.sbuf_tensor("A_dm", [128, QB], F32))
            nbhi = E(nc.sbuf_tensor("A_nbhi", [1, QB], BF16))
            nbhf = E(nc.sbuf_tensor("A_nbhf", [1, QB], F32))
            nblo = E(nc.sbuf_tensor("A_nblo", [1, QB], BF16))
            ssb = [E(nc.sbuf_tensor(f"A_ssb{i}", [128, QB], F32)) for i in range(2)]
            xres = [E(nc.sbuf_tensor(f"A_xr{i}", [128, D], F32)) for i in range(2)]
            xnew = [E(nc.sbuf_tensor(f"A_xn{i}", [128, D], F32)) for i in range(2)]
            psT = [E(nc.psum_tensor(f"A_psT{i}", [128, QB], F32)) for i in range(2)]
            psO = [E(nc.psum_tensor(f"A_psO{i}", [128, D], F32)) for i in range(2)]
            nbq = E(nc.sbuf_tensor("A_nbq", [1, QB], F32))
            vt3 = vtm[:].rearrange("p (k o) -> p k o", o=D)
            pD3 = pD[:].rearrange("p (k q) -> p k q", q=QB)
            qq3 = qqb[:].rearrange("p (c t) -> p c t", c=8)

            tok_v = pr.dma("sync", "Avtm", vt3,
                           aps["vtm_d"].rearrange("(a p) o -> p a o", p=128))
            tok_pv_last = 0
            tok_ssb_done = {0: 0, 1: 0}
            tok_xn_done = {0: 0, 1: 0}
            tok_xst = {0: None, 1: None}
            for qb in range(nqb):
                tok_qq = pr.dma("sync", "Aqqb", qqb[:].rearrange(
                    "p (c t) -> p c t", c=8),
                    qfm_r[:, :, qb * QB:(qb + 1) * QB],
                    waits=[("pe", tok_pv_last)])
                tok_nb = pr.dma("sync", "Anbq", nbq[:],
                                aps["brow_d"][0:1, qb * QB:(qb + 1) * QB],
                                waits=[("pe", tok_pv_last)])
                pr.op("vector", "tensor_copy", nbhi[:], nbq[:],
                      waits=[tok_nb, ("pe", tok_pv_last)])
                pr.op("vector", "tensor_copy", nbhf[:], nbhi[:])
                tok_nlo = pr.op("vector", "tensor_tensor", nblo[:], nbq[:],
                                nbhf[:], op=ALU.subtract)
                tok_exp = {0: 0, 1: 0}
                for kt in range(nkt):
                    b = kt % 2
                    tok_io = pr.op("gpsimd", "iota", mt[:],
                                   pattern=[[1, QB]],
                                   base=qb * QB - kt * 128,
                                   channel_multiplier=-1,
                                   allow_small_or_imprecise_dtypes=True,
                                   waits=[("ve", pr.cnt["ve"])])
                    tok_ab = pr.op("vector", "tensor_scalar",
                                   dm[:].bitcast(I32), mt[:].bitcast(I32),
                                   0x7FFFFFFF, None, op0=ALU.bitwise_and,
                                   waits=[("gc", tok_io)])

                    def stmm(te, kt=kt, b=b, tqq=tok_qq, tsd=tok_ssb_done[b],
                             tnlo=tok_nlo):
                        pr.tw(te, tqq[0], tqq[1])
                        pr.tw(te, "ve", tnlo)
                        if tsd:
                            pr.tw(te, "ve", tsd)
                        last = None
                        for nh in range(QB // 512):
                            for c in range(8):
                                te.matmul(
                                    psT[b][:, nh * 512:(nh + 1) * 512],
                                    lhsT=kfm3[:, c, kt * 128:(kt + 1) * 128],
                                    rhs=qq3[:, c, nh * 512:(nh + 1) * 512],
                                    start=(c == 0), stop=False)
                            te.matmul(psT[b][:, nh * 512:(nh + 1) * 512],
                                      lhsT=con["ones_bf_sb"][:, 0:128],
                                      rhs=nbhi[:, nh * 512:(nh + 1) * 512],
                                      start=False, stop=False)
                            last = te.matmul(
                                psT[b][:, nh * 512:(nh + 1) * 512],
                                lhsT=con["ones_bf_sb"][:, 0:128],
                                rhs=nblo[:, nh * 512:(nh + 1) * 512],
                                start=False, stop=True)
                        return last

                    tok_stm = pr._push("tensor", stmm, "pe", 1, [])
                    tok_ssb = pr.op("vector", "scalar_tensor_tensor",
                                    ssb[b][:], dm[:], LNG, psT[b][:],
                                    op0=ALU.mult, op1=ALU.add,
                                    waits=[("pe", tok_stm),
                                           ("se", tok_exp[b])])
                    tok_ssb_done[b] = tok_ssb
                    tok_exp[b] = pr.op("scalar", "activation", pD3[:, kt, :],
                                       ssb[b][:], AF.Exp,
                                       waits=[("ve", tok_ssb),
                                              ("pe", tok_pv_last)])
                for qt in range(QB // 128):
                    g = qt % 2
                    tq = qb * (QB // 128) + qt
                    tok_xr = pr.dma("sync", f"Axr{g}", xres[g][:],
                                    aps["x_in"][tq * 128:(tq + 1) * 128, :],
                                    waits=[("ve", tok_xn_done[g])])

                    def pvmm(te, qt=qt, g=g, texp=max(tok_exp.values()),
                             txn=tok_xn_done[g]):
                        pr.tw(te, "se", texp)
                        if txn:
                            pr.tw(te, "ve", txn)
                        last = None
                        for kt in range(nkt):
                            for nh in range(2):
                                last = te.matmul(
                                    psO[g][:, nh * 512:(nh + 1) * 512],
                                    lhsT=pD3[:, kt, qt * 128:(qt + 1) * 128],
                                    rhs=vt3[:, kt, nh * 512:(nh + 1) * 512],
                                    start=(kt == 0), stop=(kt == nkt - 1))
                        return last

                    tok_pv = pr._push("tensor", pvmm, "pe", 1,
                                      [tok_v, tok_xr])
                    tok_pv_last = tok_pv
                    tok_xn = pr.op("vector", "tensor_tensor", xnew[g][:],
                                   psO[g][:], xres[g][:], op=ALU.add,
                                   waits=[("pe", tok_pv), tok_xr, tok_xst[g]])
                    tok_xn_done[g] = tok_xn
                    tok_xst[g] = pr.dma("sync", f"Axst{g}",
                                        aps["x_out"][tq * 128:(tq + 1) * 128, :],
                                        xnew[g][:], waits=[("ve", tok_xn)])
            pr.barrier()


CAP = 768
BIGIDX = 1 << 20


def emit_phase_B(pr, nc, T, aps, con):
    from concourse.bass import IndirectOffsetOnAxis
    nqt = T // 128
    cap = min(CAP, T)
    ncap = NE * cap
    rb_cap = nc.gpsimd.alloc_register()
    rb_tok = nc.gpsimd.alloc_register()
    pr.op("gpsimd", "reg_mov", rb_cap, ncap - 1)
    pr.op("gpsimd", "reg_mov", rb_tok, T - 1)
    with contextlib.ExitStack() as stP:
        EP = stP.enter_context
        consts_sb(pr, nc, stP, aps, con,
                  need=("ident_bf", "ident_f32", "ones_bf", "ltri"))
        sel_all = EP(nc.sbuf_tensor("B_sel_all", [128, NE * nqt], F32))
        wtok = EP(nc.sbuf_tensor("B_wtok", [128, nqt], F32))
        ivx = EP(nc.sbuf_tensor("B_ivx", [128, NE * (cap // 128)], U32))
        sel4 = sel_all[:].rearrange("p (e t) -> p e t", e=NE)

        with contextlib.ExitStack() as st:
            E = st.enter_context
            xlnT = E(nc.sbuf_tensor("B_xlnT", [128, 8 * T], BF16))
            r = dict(
                wr_sb=E(nc.sbuf_tensor("B_wr", [128, 8 * NE], BF16)),
                br_sb=E(nc.sbuf_tensor("B_br", [1, NE], BF16)),
                psr=[E(nc.psum_tensor(f"B_psr{i}", [128, NE], F32))
                     for i in range(2)],
                negmx=E(nc.sbuf_tensor("B_ngm", [128, 1], F32)),
                mx=E(nc.sbuf_tensor("B_mx", [128, 1], F32)),
                er=E(nc.sbuf_tensor("B_er", [128, NE], F32)),
                z8=E(nc.sbuf_tensor("B_z8", [128, 1], F32)),
                g8=E(nc.sbuf_tensor("B_g8", [128, 1], F32)),
                sel=E(nc.sbuf_tensor("B_sel", [128, NE], F32)),
                mi=E(nc.sbuf_tensor("B_mi", [128, NE], F32)),
                idxm=E(nc.sbuf_tensor("B_idxm", [128, 1], F32)),
                idxp=E(nc.sbuf_tensor("B_idxp", [128, 1], F32)),
                self_=E(nc.sbuf_tensor("B_self", [128, NE], F32)),
                e_bc=E(nc.sbuf_tensor("B_ebc", [128, NE], F32)),
                sel4=sel4, wtok=wtok,
                tok_gate=[0, 0],
            )
            r["tok_wr"] = pr.dma(
                "gpsimd", "Bwr",
                r["wr_sb"][:].rearrange("p (c e) -> p c e", c=8),
                aps["Wr"].rearrange("(c p) e -> p c e", p=128))
            r["tok_br"] = pr.dma("gpsimd", "Bbr", r["br_sb"][:],
                                 aps["br"].rearrange("(o e) -> o e", o=1))
            r["tok_ebc"] = pr.op("gpsimd", "iota", r["e_bc"][:],
                                 pattern=[[1, NE]], base=0,
                                 channel_multiplier=0,
                                 allow_small_or_imprecise_dtypes=True)
            emit_ln_pass(pr, st, nc, T, aps["x_in"], xlnT, con, "Bln",
                         router=r, xln_tm_dram=aps["xln_tm_d"])

        with contextlib.ExitStack() as st:
            E = st.enter_context
            pexA = E(nc.sbuf_tensor("B_pexA", [128, NE * nqt], F32))
            pexB = E(nc.sbuf_tensor("B_pexB", [128, NE * nqt], F32))
            cnt = E(nc.sbuf_tensor("B_cnt", [128, NE], F32))
            cntb = E(nc.sbuf_tensor("B_cntb", [128, NE], BF16))
            base = E(nc.sbuf_tensor("B_base", [128, NE], F32))
            ecap = E(nc.sbuf_tensor("B_ecap", [128, NE], F32))
            S1 = E(nc.sbuf_tensor("B_S1", [128, NE], F32))
            M2 = E(nc.sbuf_tensor("B_M2", [128, nqt * NE], F32))
            RK = E(nc.sbuf_tensor("B_RK", [128, nqt * NE], F32))
            slots = E(nc.sbuf_tensor("B_slots", [128, nqt], F32))
            rankt = E(nc.sbuf_tensor("B_rank", [128, nqt], F32))
            pen = E(nc.sbuf_tensor("B_pen", [128, nqt], F32))
            slots_u = E(nc.sbuf_tensor("B_slots_u", [128, nqt], U32))
            tki = E(nc.sbuf_tensor("B_tki", [128, nqt], F32))
            tki_u = E(nc.sbuf_tensor("B_tki_u", [128, nqt], U32))
            fillb = E(nc.sbuf_tensor("B_fill", [128, ncap // 128], U32))
            wt32 = E(nc.sbuf_tensor("B_wt32", [nqt, 128], F32))
            zeroc = E(nc.sbuf_tensor("B_zero", [128, D], F32))
            psb = E(nc.psum_tensor("B_psb", [128, 128], F32))
            pA = pexA[:].rearrange("p (e t) -> p e t", e=NE)
            pB = pexB[:].rearrange("p (e t) -> p e t", e=NE)
            M2v = M2[:].rearrange("p (t e) -> p t e", e=NE)
            RKv = RK[:].rearrange("p (t e) -> p t e", e=NE)

            pr.op("vector", "memset", pexA[:], 0.0)
            pr.op("vector", "tensor_copy", pA[:, :, 1:nqt],
                  sel4[:, :, 0:nqt - 1])
            cur, nxt = (pexA, pA), (pexB, pB)
            s = 1
            while s < nqt:
                pr.op("vector", "tensor_copy", nxt[1][:, :, 0:s],
                      cur[1][:, :, 0:s])
                pr.op("vector", "tensor_tensor", nxt[1][:, :, s:nqt],
                      cur[1][:, :, s:nqt], cur[1][:, :, 0:nqt - s],
                      op=ALU.add)
                cur, nxt = nxt, cur
                s *= 2
            pex = cur[1]
            pr.op("vector", "tensor_reduce", cnt[:], sel4, axis=AX.X,
                  op=ALU.add)
            pr.op("vector", "tensor_copy", cntb[:], cnt[:])
            tok_cnt = pr.cnt["ve"]
            tok_bm = pr.mm(lambda te: te.matmul(psb[:, 0:NE],
                                                lhsT=con["ltri_sb"][:],
                                                rhs=cntb[:], start=True,
                                                stop=True),
                           waits=[("ve", tok_cnt)])
            pr.op("vector", "tensor_copy", base[:], psb[:, 0:NE],
                  waits=[("pe", tok_bm)])
            pr.op("gpsimd", "iota", ecap[:], pattern=[[cap, NE]], base=0,
                  channel_multiplier=0, allow_small_or_imprecise_dtypes=True)
            pr.op("vector", "tensor_tensor", S1[:], base[:], ecap[:],
                  op=ALU.add, waits=[("gc", pr.cnt["gc"])])
            for e in range(NE):
                pr.op("vector", "scalar_tensor_tensor", M2v[:, :, e],
                      pex[:, e, :], S1[:, e:e + 1], sel4[:, e, :],
                      op0=ALU.add, op1=ALU.mult)
                pr.op("vector", "scalar_tensor_tensor", RKv[:, :, e],
                      pex[:, e, :], base[:, e:e + 1], sel4[:, e, :],
                      op0=ALU.add, op1=ALU.mult)
            pr.op("vector", "tensor_reduce", slots[:], M2v, axis=AX.X,
                  op=ALU.add)
            pr.op("vector", "tensor_reduce", rankt[:], RKv, axis=AX.X,
                  op=ALU.add)
            pr.op("vector", "tensor_scalar", pen[:], rankt[:], cap - 0.5,
                  None, op0=ALU.is_ge)
            pr.op("vector", "scalar_tensor_tensor", slots[:], pen[:],
                  float(BIGIDX), slots[:], op0=ALU.mult, op1=ALU.add)
            pr.op("vector", "tensor_copy", slots_u[:], slots[:])
            pr.op("gpsimd", "iota", tki[:], pattern=[[128, nqt]], base=0,
                  channel_multiplier=1, allow_small_or_imprecise_dtypes=True)
            tok_rt = pr.op("vector", "tensor_copy", tki_u[:], tki[:],
                           waits=[("gc", pr.cnt["gc"])])
            tok_fl = pr.op("vector", "memset", fillb[:], float(BIGIDX))
            tok_fd = pr.dma("sync", "Binv", 
                            aps["inv_d"][:, 0].rearrange(
                                "(p o) -> p o", o=ncap // 128),
                            fillb[:], waits=[("ve", tok_fl)])
            tok_sc = None
            for t in range(nqt):
                tok_sc = pr.idma(
                    "Bscat", aps["inv_d"],
                    IndirectOffsetOnAxis(ap=slots_u[:, t:t + 1], axis=0),
                    tki_u[:, t:t + 1], None,
                    bounds_check=rb_cap, oob_is_err=False,
                    waits=[("ve", tok_rt), tok_fd])
            tok_wt = pr.mm(lambda te: te.transpose(
                psb[0:nqt, 0:128], wtok[:], con["ident_f32_sb"][:]),
                waits=[("ve", pr.cnt["ve"]), ("pe", tok_bm)])
            tok_wc = pr.op("vector", "tensor_copy", wt32[:],
                           psb[0:nqt, 0:128], waits=[("pe", tok_wt)])
            pr.dma("sync", "Bwd",
                   aps["w_d"][:, 0].rearrange("(a b) -> a b", a=nqt),
                   wt32[:], waits=[("ve", tok_wc)])
            tok_z = pr.op("vector", "memset", zeroc[:], 0.0)
            for t in range(nqt):
                pr.dma("sync", "Bysz",
                       aps["ys_d"][t * 128:(t + 1) * 128, :], zeroc[:],
                       waits=[("ve", tok_z)])
            pr.dma("gpsimd", "Bivx", ivx[:],
                   aps["inv_d"][:, 0].rearrange("(c p) -> p c", p=128),
                   waits=[tok_sc], allow_slow_non_contiguous=True)
            pr.barrier()

        with contextlib.ExitStack() as st:
            E = st.enter_context
            W2sb = E(nc.sbuf_tensor("B_W2", [128, 32 * D], BF16))
            hT = E(nc.sbuf_tensor("B_hT", [128, 32 * cap], BF16))
            xgT = E(nc.sbuf_tensor("B_xgT", [128, 8 * cap], BF16))
            xg = [E(nc.sbuf_tensor(f"B_xg{i}", [128, D], BF16))
                  for i in range(2)]
            w1b = [E(nc.sbuf_tensor(f"B_w1b{i}", [128, 8 * 512], BF16))
                   for i in range(2)]
            b1sq = E(nc.sbuf_tensor("B_b1sq", [32, 128], F32))
            b1col = E(nc.sbuf_tensor("B_b1c", [128, 32], F32))
            b2row = E(nc.sbuf_tensor("B_b2r", [1, D], BF16))
            gw = E(nc.sbuf_tensor("B_gw", [128, NE * (cap // 128)], F32))
            yo = [E(nc.sbuf_tensor(f"B_yo{i}", [128, D], F32))
                  for i in range(2)]
            psH = [E(nc.psum_tensor(f"B_psH{i}", [128, cap], F32))
                   for i in range(2)]
            psY = E(nc.psum_tensor("B_psY", [128, D], F32))
            psX = [E(nc.psum_tensor(f"B_psX{i}", [128, D], BF16))
                   for i in range(2)]
            W2r = W2sb[:].rearrange("p (hc o) -> p hc o", o=D)
            hTr = hT[:].rearrange("p (hc t) -> p hc t", t=cap)
            xgT3 = xgT[:].rearrange("p (c t) -> p c t", c=8)
            nhc = FF0 // 128
            njt = cap // 128

            pr.op("vector", "memset", xg[0][:], 0.0)
            pr.op("vector", "memset", xg[1][:], 0.0)
            pr.barrier()
            tok_mm2_last = 0
            tok_mm1_last = 0
            tok_gel = [0, 0]
            tok_w1_rd = [0, 0]
            tok_yst = [None, None]
            tok_yrd = [0, 0]
            tok_b1_rd = 0
            tok_gw_rd = 0
            tok_xg_rd = [0, 0]
            tok_xgT_wr = 0
            for e in range(NE):
                tok_w2 = pr.dma("gpsimd", "BW2", W2r,
                                aps["W2"][e].rearrange("(hc p) o -> p hc o",
                                                       p=128),
                                waits=[("pe", tok_mm2_last)])
                tok_b2 = pr.dma("gpsimd", "Bb2", b2row[:],
                                aps["b2"][e].rearrange("(o d) -> o d", o=1),
                                waits=[("pe", tok_mm2_last)])
                tok_b1s = pr.dma("sync", "Bb1s", b1sq[:],
                                 aps["b1"][e].rearrange("(a b) -> a b", a=32),
                                 waits=[("pe", tok_b1_rd)])
                tok_b1t = pr.mm(lambda te: te.transpose(
                    psY[:, 0:32], b1sq[:], con["ident_f32_sb"][0:32, 0:32]),
                    waits=[tok_b1s, ("ve", tok_yrd[0]), ("ve", tok_yrd[1])])
                tok_b1_rd = tok_b1t
                tok_b1c = pr.op("vector", "tensor_copy", b1col[:],
                                psY[:, 0:32], waits=[("pe", tok_b1t)])
                tok_gws = None
                for j in range(njt):
                    tok_gws = pr.idma(
                        "Bgw", gw[:, e * njt + j:e * njt + j + 1], None,
                        aps["w_d"],
                        IndirectOffsetOnAxis(
                            ap=ivx[:, e * njt + j:e * njt + j + 1], axis=0),
                        bounds_check=rb_tok, oob_is_err=False,
                        waits=[("ve", tok_gw_rd)])
                tok_tpx = 0
                tok_ev = 0
                for j in range(njt):
                    b = j % 2
                    tok_g = pr.idma(
                        "Bxg" + str(b), xg[b][:], None,
                        aps["xln_tm_d"],
                        IndirectOffsetOnAxis(
                            ap=ivx[:, e * njt + j:e * njt + j + 1], axis=0),
                        bounds_check=rb_tok, oob_is_err=False,
                        waits=[("pe", tok_xg_rd[b])])

                    def tpx(te, b=b, tok_g=tok_g, tev=tok_ev,
                            tm1=tok_mm1_last):
                        pr.tw(te, tok_g[0], tok_g[1])
                        if tev:
                            pr.tw(te, "ve", tev)
                        if tm1:
                            pr.tw(te, "pe", tm1)
                        last = None
                        for c in range(8):
                            last = te.transpose(
                                psX[b][:, c * 128:(c + 1) * 128],
                                xg[b][:, c * 128:(c + 1) * 128],
                                con["ident_bf_sb"][:])
                        return last

                    tok_tpx = pr._push("tensor", tpx, "pe", 1, [])
                    tok_xg_rd[b] = tok_tpx
                    tok_ev = pr.op("vector", "tensor_copy",
                                   xgT3[:, :, j * 128:(j + 1) * 128],
                                   psX[b][:].rearrange("p (c t) -> p c t",
                                                       c=8),
                                   waits=[("pe", tok_tpx)])
                tok_xgT_wr = tok_ev
                for hcb in range(nhc // 4):
                    wpar = hcb % 2
                    w13 = w1b[wpar][:].rearrange("p (c h) -> p c h", c=8)
                    tok_w1 = pr.dma(
                        "gpsimd", f"Bw1{wpar}", w13,
                        aps["W1"][e][:, hcb * 512:(hcb + 1) * 512]
                        .rearrange("(c p) h -> p c h", p=128),
                        waits=[("pe", tok_w1_rd[wpar])])
                    for hc4 in range(4):
                        hc = hcb * 4 + hc4
                        hpar = hc % 2

                        def m1(te, hc4=hc4, w13=w13, hpar=hpar,
                               tok_w1=tok_w1, tgel=tok_gel[hpar],
                               tb1=tok_b1c, txg=tok_xgT_wr):
                            pr.tw(te, tok_w1[0], tok_w1[1])
                            pr.tw(te, "ve", max(tb1, txg))
                            if tgel:
                                pr.tw(te, "se", tgel)
                            last = None
                            for lo in range(0, cap, 512):
                                hi = min(lo + 512, cap)
                                for c in range(8):
                                    last = te.matmul(
                                        psH[hpar][:, lo:hi],
                                        lhsT=w13[:, c,
                                                 hc4 * 128:(hc4 + 1) * 128],
                                        rhs=xgT3[:, c, lo:hi],
                                        start=(c == 0), stop=(c == 7))
                            return last

                        tok_m1 = pr._push("tensor", m1, "pe", 1, [])
                        tok_w1_rd[wpar] = tok_m1
                        tok_mm1_last = tok_m1
                        tok_gel[hpar] = pr.op(
                            "scalar", "activation", hTr[:, hc, :],
                            psH[hpar][:], AF.Gelu,
                            bias=b1col[:, hc:hc + 1],
                            waits=[("pe", tok_m1), ("pe", tok_mm2_last)])
                for tc in range(njt):
                    ypar = tc % 2

                    def m2(te, tc=tc, tok_w2=tok_w2, tok_b2=tok_b2,
                           tgel=pr.cnt["se"], tyrd=max(tok_yrd)):
                        pr.tw(te, tok_w2[0], tok_w2[1])
                        pr.tw(te, tok_b2[0], tok_b2[1])
                        pr.tw(te, "se", tgel)
                        if tyrd:
                            pr.tw(te, "ve", tyrd)
                        last = None
                        for hc in range(nhc):
                            for nh in range(2):
                                last = te.matmul(
                                    psY[:, nh * 512:(nh + 1) * 512],
                                    lhsT=hTr[:, hc, tc * 128:(tc + 1) * 128],
                                    rhs=W2r[:, hc, nh * 512:(nh + 1) * 512],
                                    start=(hc == 0), stop=False)
                        for nh in range(2):
                            last = te.matmul(
                                psY[:, nh * 512:(nh + 1) * 512],
                                lhsT=con["ones_bf_sb"][:, 0:128],
                                rhs=b2row[:, nh * 512:(nh + 1) * 512],
                                start=False, stop=True)
                        return last

                    tok_m2 = pr._push("tensor", m2, "pe", 1, [])
                    tok_mm2_last = tok_m2
                    tok_ye = pr.op("vector", "tensor_scalar", yo[ypar][:],
                                   psY[:], gw[:, e * njt + tc:e * njt + tc + 1],
                                   None, op0=ALU.mult,
                                   waits=[("pe", tok_m2), tok_gws,
                                          tok_yst[ypar]])
                    tok_yrd[ypar] = tok_ye
                    tok_gw_rd = tok_ye
                    tok_yst[ypar] = pr.idma(
                        "Byst" + str(ypar), aps["ys_d"],
                        IndirectOffsetOnAxis(
                            ap=ivx[:, e * njt + tc:e * njt + tc + 1], axis=0),
                        yo[ypar][:], None,
                        bounds_check=rb_tok, oob_is_err=False,
                        waits=[("ve", tok_ye)])
            pr.barrier()

        with contextlib.ExitStack() as st:
            E = st.enter_context
            xa = [E(nc.sbuf_tensor(f"B_xa{i}", [128, D], F32))
                  for i in range(2)]
            yb = [E(nc.sbuf_tensor(f"B_yb{i}", [128, D], F32))
                  for i in range(2)]
            xo = [E(nc.sbuf_tensor(f"B_xo{i}", [128, D], F32))
                  for i in range(2)]
            tok_rd = [0, 0]
            tok_st = [None, None]
            for t in range(nqt):
                b = t % 2
                ta = pr.dma("sync", f"Bxa{b}", xa[b][:],
                            aps["x_in"][t * 128:(t + 1) * 128, :],
                            waits=[("ve", tok_rd[b])])
                tb = pr.dma("sync", f"Byb{b}", yb[b][:],
                            aps["ys_d"][t * 128:(t + 1) * 128, :],
                            waits=[("ve", tok_rd[b])])
                tok_rd[b] = pr.op("vector", "tensor_tensor", xo[b][:],
                                  xa[b][:], yb[b][:], op=ALU.add,
                                  waits=[ta, tb, tok_st[b]])
                tok_st[b] = pr.dma("sync", f"Bxo{b}",
                                   aps["x_out"][t * 128:(t + 1) * 128, :],
                                   xo[b][:], waits=[("ve", tok_rd[b])])
            pr.barrier()


def emit_prep_l1attn(pr, nc, aps, con):
    specs = [
        ("wq", aps["wq"], aps["wqT_d"], True),
        ("wk_a", aps["wk_a"], aps["wkaT_d"], False),
        ("wk_b", aps["wk_b"], aps["wkbT_d"], True),
        ("wv_a", aps["wv_a"], aps["wvaT_d"], False),
        ("wv_b", aps["wv_b"], aps["wvbT_d"], False),
        ("wo", aps["wo"], aps["woT_d"], False),
    ]
    con = dict((k, v) for k, v in con.items() if not k.endswith("_sb"))
    with contextlib.ExitStack() as st:
        E = st.enter_context
        consts_sb(pr, nc, st, aps, con,
                  need=("ident_bf", "ident_f32", "ropeC", "ropeS"))
        wsb = E(nc.sbuf_tensor("P_wsb", [128, 8 * D], BF16))
        pstp = [E(nc.psum_tensor(f"P_ps{i}", [128, D], BF16)) for i in range(2)]
        t1 = E(nc.sbuf_tensor("P_t1", [128, D], F32))
        sw = E(nc.sbuf_tensor("P_sw", [128, D], F32))
        ob = [E(nc.sbuf_tensor(f"P_ob{i}", [128, D], BF16)) for i in range(2)]
        tok_stp = [None, None]
        tok_ev = [0, 0]
        for name, src, dst, fold in specs:
            O, I = src.shape
            na, nch = O // 128, I // 128
            w3 = wsb[:, 0:na * I].rearrange("p (a i) -> p a i", a=na)
            tok_w = pr.dma("gpsimd", "Pw", w3,
                           src.rearrange("(a p) i -> p a i", p=128),
                           waits=[("pe", pr.cnt["pe"]),
                                  ("ve", max(tok_ev)),
                                  ("se", pr.cnt["se"])])
            for c in range(nch):
                b = c % 2

                def tpf(te, c=c, b=b, na=na, w3=w3, tok_w=tok_w,
                        tev=tok_ev[b]):
                    pr.tw(te, tok_w[0], tok_w[1])
                    if tev:
                        pr.tw(te, "ve", tev)
                    last = None
                    for a in range(na):
                        last = te.transpose(
                            pstp[b][:, a * 128:(a + 1) * 128],
                            w3[:, a, c * 128:(c + 1) * 128],
                            con["ident_bf_sb"][:])
                    return last

                tok_tp = pr._push("tensor", tpf, "pe", 1, [])
                if fold:
                    pr.op("vector", "tensor_tensor", t1[:, 0:O],
                          pstp[b][:, 0:O], con["ropeC_sb"][:, 0:O],
                          op=ALU.mult, waits=[("pe", tok_tp)])
                    vps = pstp[b][:, 0:O].rearrange("p (n two) -> p n two",
                                                    two=2)
                    vsw = sw[:, 0:O].rearrange("p (n two) -> p n two", two=2)
                    pr.op("scalar", "copy", vsw[:, :, 0:1], vps[:, :, 1:2],
                          waits=[("pe", tok_tp), ("ve", pr.cnt["ve"])])
                    tok_sc = pr.op("scalar", "copy", vsw[:, :, 1:2],
                                   vps[:, :, 0:1])
                    pr.op("vector", "tensor_tensor", sw[:, 0:O], sw[:, 0:O],
                          con["ropeS_sb"][:, 0:O], op=ALU.mult,
                          waits=[("se", tok_sc)])
                    tok_ev[b] = pr.op("vector", "tensor_tensor", ob[b][:, 0:O],
                                      t1[:, 0:O], sw[:, 0:O], op=ALU.add,
                                      waits=[tok_stp[b]])
                else:
                    tok_ev[b] = pr.op("vector", "tensor_copy", ob[b][:, 0:O],
                                      pstp[b][:, 0:O],
                                      waits=[("pe", tok_tp), tok_stp[b]])
                tok_stp[b] = pr.dma("sync", f"Pst{b}",
                                    dst[c * 128:(c + 1) * 128, 0:O],
                                    ob[b][:, 0:O], waits=[("ve", tok_ev[b])])
        pr.barrier()


def emit_prep_l1ffn(pr, nc, aps, con):
    specs = [
        ("wg", aps["w_gate"], aps["wgT_d"]),
        ("wu", aps["w_up"], aps["wuT_d"]),
        ("wd", aps["w_down"], aps["wdT_d"]),
    ]
    con = dict((k, v) for k, v in con.items() if not k.endswith("_sb"))
    with contextlib.ExitStack() as stc:
        consts_sb(pr, nc, stc, aps, con, need=("ident_bf",))
        for name, src, dst in specs:
            O, I = src.shape
            na, nch = O // 128, I // 128
            with contextlib.ExitStack() as st:
                E = st.enter_context
                wi = [E(nc.sbuf_tensor(f"P_{name}_i{i}", [128, I], I32))
                      for i in range(2)]
                wdq = [E(nc.sbuf_tensor(f"P_{name}_q{i}", [128, I], BF16))
                       for i in range(2)]
                pst = [E(nc.psum_tensor(f"P_{name}_ps{i}", [128, nch * 128],
                                        BF16)) for i in range(2)]
                asm = E(nc.sbuf_tensor(f"P_{name}_T", [128, nch * O], BF16))
                asm3 = asm[:].rearrange("p (c o) -> p c o", c=nch)
                tok_dq = [0, 0]
                tok_ev = [0, 0]
                for a in range(na):
                    b = a % 2
                    tok_i = pr.dma("sync", f"Pwi{b}", wi[b][:],
                                   src[a * 128:(a + 1) * 128, :],
                                   waits=[("ve", tok_dq[b])])
                    tok_dq[b] = pr.op("vector", "tensor_scalar", wdq[b][:],
                                      wi[b][:], DEQ, None, op0=ALU.mult,
                                      waits=[tok_i, ("pe", pr.cnt["pe"])])

                    def tpf(te, a=a, b=b, nch=nch, tdq=tok_dq[b],
                            tev=tok_ev[b]):
                        pr.tw(te, "ve", tdq)
                        if tev:
                            pr.tw(te, "ve", tev)
                        last = None
                        for c in range(nch):
                            last = te.transpose(
                                pst[b][:, c * 128:(c + 1) * 128],
                                wdq[b][:, c * 128:(c + 1) * 128],
                                con["ident_bf_sb"][:])
                        return last

                    tok_tp = pr._push("tensor", tpf, "pe", 1, [])
                    tok_ev[b] = pr.op(
                        "vector", "tensor_copy",
                        asm3[:, :, a * 128:(a + 1) * 128],
                        pst[b][:].rearrange("p (c o) -> p c o", c=nch),
                        waits=[("pe", tok_tp)])
                pr.dma("sync", "PstT", dst.rearrange("(c p) o -> p c o", p=128),
                       asm3, waits=[("ve", max(tok_ev))])
                pr.barrier()


def emit_phase_C(pr, nc, T, aps, con):
    QG = min(512, T)
    nqg = T // QG
    nkt = T // 128
    emit_prep_l1attn(pr, nc, aps, con)
    with contextlib.ExitStack() as stP:
        EP = stP.enter_context
        consts_sb(pr, nc, stP, aps, con,
                  need=("ident_bf", "ident_f32", "ones_f32", "headmask2"))
        v_all = EP(nc.sbuf_tensor("C_vall", [128, nkt * NHEAD * (HD + 1)], BF16))
        bias_all = EP(nc.sbuf_tensor("C_bias", [128, NHEAD], F32))
        v4 = v_all[:].rearrange("p (k h e) -> p k h e", h=NHEAD, e=HD + 1)
        qfm = aps["qfm1_d"]
        kfm = aps["kfm1_d"]

        with contextlib.ExitStack() as st:
            E = st.enter_context
            xlnT = E(nc.sbuf_tensor("C_xlnT", [128, 8 * T], BF16))
            with contextlib.ExitStack() as stln:
                emit_ln_pass(pr, stln, nc, T, aps["x_in"], xlnT, con, "Cln")
            xlnT3 = xlnT[:].rearrange("p (c t) -> p c t", c=8)

            wqT = E(nc.sbuf_tensor("C_wqT", [128, 8 * D], BF16))
            wkaT = E(nc.sbuf_tensor("C_wkaT", [128, 8 * LATD], BF16))
            wkbT = E(nc.sbuf_tensor("C_wkbT", [128, 2 * D], BF16))
            wvaT = E(nc.sbuf_tensor("C_wvaT", [128, 8 * LATD], BF16))
            wvbT = E(nc.sbuf_tensor("C_wvbT", [128, 2 * D], BF16))
            ck = E(nc.sbuf_tensor("C_ck", [128, 2 * T], BF16))
            cv = E(nc.sbuf_tensor("C_cv", [128, 2 * T], BF16))
            qfb = [E(nc.sbuf_tensor(f"C_qf{i}", [128, 512], BF16))
                   for i in range(2)]
            sqb = E(nc.sbuf_tensor("C_sqb", [128, 512], F32))
            t21 = E(nc.sbuf_tensor("C_t21", [2, 1], F32))
            mqq = E(nc.sbuf_tensor("C_mqq", [2, 8], F32))
            mkk = E(nc.sbuf_tensor("C_mkk", [2, 8], F32))
            nbg = E(nc.sbuf_tensor("C_nbg", [2, 8], F32))
            nbgr = E(nc.sbuf_tensor("C_nbgr", [1, NHEAD], F32))
            psq = [E(nc.psum_tensor(f"C_psq{i}", [128, 512], F32))
                   for i in range(2)]
            psv = E(nc.psum_tensor("C_psv", [128, D], F32))
            psN = E(nc.psum_tensor("C_psN", [2, 512], F32))
            psb = E(nc.psum_tensor("C_psb", [128, 1], F32))

            tok_cst = []
            for name, h in (("wqT_d", wqT), ("wkaT_d", wkaT),
                            ("wkbT_d", wkbT), ("wvaT_d", wvaT),
                            ("wvbT_d", wvbT)):
                w, cols = aps[name].shape
                tok_cst.append(pr.dma(
                    "sync", "Cw",
                    h[:, 0:(w // 128) * cols].rearrange(
                        "p (c o) -> p c o", c=w // 128),
                    aps[name].rearrange("(c p) o -> p c o", p=128)))
            pr.op("vector", "memset", mqq[:], 0.0)
            pr.op("vector", "memset", mkk[:], 0.0)
            pr.op("vector", "memset", v4[:, :, :, HD:HD + 1], 1.0)
            tok_w_last = tok_cst[-1]

            wkaT3 = wkaT[:].rearrange("p (c l) -> p c l", c=8)
            wvaT3 = wvaT[:].rearrange("p (c l) -> p c l", c=8)
            for which, wT3, cdst in (("k", wkaT3, ck), ("v", wvaT3, cv)):
                cd3 = cdst[:].rearrange("p (l t) -> p l t", l=2)
                for lc in range(2):
                    for nb in range(T // 512):
                        b = (lc * (T // 512) + nb) % 2

                        def cmm(te, lc=lc, nb=nb, b=b, wT3=wT3,
                                tw=tok_w_last, tev=pr.cnt["ve"]):
                            pr.tw(te, tw[0], tw[1])
                            pr.tw(te, "ve", tev)
                            last = None
                            for c in range(8):
                                last = te.matmul(
                                    psq[b][:],
                                    lhsT=wT3[:, c, lc * 128:(lc + 1) * 128],
                                    rhs=xlnT3[:, c, nb * 512:(nb + 1) * 512],
                                    start=(c == 0), stop=(c == 7))
                            return last

                        tok_c = pr._push("tensor", cmm, "pe", 1, [])
                        pr.op("vector", "tensor_copy",
                              cd3[:, lc, nb * 512:(nb + 1) * 512], psq[b][:],
                              waits=[("pe", tok_c)])
            pr.barrier()

            wqT3 = wqT[:].rearrange("p (c o) -> p c o", c=8)
            wkbT3 = wkbT[:].rearrange("p (l o) -> p l o", l=2)
            ck3 = ck[:].rearrange("p (l t) -> p l t", l=2)
            tok_st = [None, None]
            tok_nprev = 0
            for which in ("q", "k"):
                dstd = qfm if which == "q" else kfm
                macc = mqq if which == "q" else mkk
                for oc in range(8):
                    for nb in range(T // 512):
                        b = (oc * (T // 512) + nb) % 2

                        def pmm(te, oc=oc, nb=nb, b=b, which=which,
                                trd=pr.cnt["ve"], tse=pr.cnt["se"]):
                            pr.tw(te, "ve", trd)
                            pr.tw(te, "se", tse)
                            last = None
                            if which == "q":
                                for c in range(8):
                                    last = te.matmul(
                                        psq[b][:],
                                        lhsT=wqT3[:, c, oc * 128:(oc + 1) * 128],
                                        rhs=xlnT3[:, c, nb * 512:(nb + 1) * 512],
                                        start=(c == 0), stop=(c == 7))
                            else:
                                for lc in range(2):
                                    last = te.matmul(
                                        psq[b][:],
                                        lhsT=wkbT3[:, lc, oc * 128:(oc + 1) * 128],
                                        rhs=ck3[:, lc, nb * 512:(nb + 1) * 512],
                                        start=(lc == 0), stop=(lc == 1))
                            return last

                        tok_p = pr._push("tensor", pmm, "pe", 1, [])
                        tok_ev = pr.op("vector", "tensor_copy", qfb[b][:],
                                       psq[b][:],
                                       waits=[("pe", tok_p), tok_st[b]])
                        tok_st[b] = pr.dma(
                            "sync", f"Cqst{b}",
                            dstd[oc * 128:(oc + 1) * 128,
                                 nb * 512:(nb + 1) * 512],
                            qfb[b][:], waits=[("ve", tok_ev)])
                        tok_sq = pr.op("scalar", "activation", sqb[:],
                                       psq[b][:], AF.Square,
                                       waits=[("ve", tok_ev),
                                              ("pe", tok_nprev)])

                        def nmm(te, tok_sq=tok_sq, tve=pr.cnt["ve"]):
                            pr.tw(te, "se", tok_sq)
                            pr.tw(te, "ve", tve)
                            return te.matmul(psN[:], lhsT=con["headmask2_sb"][:],
                                             rhs=sqb[:], start=True, stop=True)

                        tok_nprev = pr._push("tensor", nmm, "pe", 1, [])
                        pr.op("vector", "tensor_reduce", t21[:], psN[:],
                              axis=AX.X, op=ALU.max,
                              waits=[("pe", tok_nprev)])
                        pr.op("vector", "tensor_tensor", macc[:, oc:oc + 1],
                              macc[:, oc:oc + 1], t21[:], op=ALU.max)
            pr.barrier()

            wvbT3 = wvbT[:].rearrange("p (l o) -> p l o", l=2)
            cv3 = cv[:].rearrange("p (l t) -> p l t", l=2)
            for tc in range(nkt):
                def vmm(te, tc=tc, trd=pr.cnt["ve"]):
                    pr.tw(te, "ve", trd)
                    last = None
                    for nh in range(2):
                        for lc in range(2):
                            last = te.matmul(
                                psv[:, nh * 512:(nh + 1) * 512],
                                lhsT=cv3[:, lc, tc * 128:(tc + 1) * 128],
                                rhs=wvbT3[:, lc, nh * 512:(nh + 1) * 512],
                                start=(lc == 0), stop=(lc == 1))
                    return last

                tok_v = pr._push("tensor", vmm, "pe", 1, [])
                pr.op("vector", "tensor_copy", v4[:, tc, :, 0:HD],
                      psv[:].rearrange("p (h d) -> p h d", h=NHEAD),
                      waits=[("pe", tok_v)])
            pr.op("vector", "tensor_tensor", nbg[:], mqq[:], mkk[:],
                  op=ALU.mult)
            tok_sq2 = pr.op("scalar", "activation", nbg[:], nbg[:], AF.Sqrt,
                            scale=1.0 / 64.0, waits=[("ve", pr.cnt["ve"])])
            tok_nbg = pr.op("vector", "tensor_scalar", nbg[:], nbg[:], 1.0,
                            -1.0, op0=ALU.add, op1=ALU.mult,
                            waits=[("se", tok_sq2)])
            tok_bd = pr.dma("sync", "Cnbg",
                            aps["nbg_d"][0, :].rearrange("(o s) -> s o", s=2),
                            nbg[:], waits=[("ve", tok_nbg)],
                            allow_slow_non_contiguous=True)
            tok_br = pr.dma("sync", "Cnbg", nbgr[:], aps["nbg_d"][0:1, :],
                            waits=[tok_bd])
            for h in range(NHEAD):
                tok_bb = pr.mm(
                    lambda te, h=h: te.matmul(psb[:],
                                              lhsT=con["ones_f32_sb"][:],
                                              rhs=nbgr[:, h:h + 1],
                                              start=True, stop=True),
                    waits=[tok_br, ("se", pr.cnt["se"])])
                pr.op("scalar", "copy", bias_all[:, h:h + 1], psb[:],
                      waits=[("pe", tok_bb)])
            pr.barrier()

        with contextlib.ExitStack() as st:
            E = st.enter_context
            kh2 = [E(nc.sbuf_tensor(f"C_kh{i}", [128, T], BF16))
                   for i in range(2)]
            qh2 = [E(nc.sbuf_tensor(f"C_qh{i}", [128, QG], BF16))
                   for i in range(2)]
            pbuf = [E(nc.sbuf_tensor(f"C_pb{i}", [128, QG], BF16))
                    for i in range(6)]
            zrec = E(nc.sbuf_tensor("C_zrec", [128, 4], F32))
            obuf = E(nc.sbuf_tensor("C_obuf", [128, 4 * HD], BF16))
            otmp = E(nc.sbuf_tensor("C_otmp", [64, QG], BF16))
            Ofm = E(nc.sbuf_tensor("C_Ofm", [128, 8 * QG], BF16))
            woT = E(nc.sbuf_tensor("C_woT", [128, 8 * D], BF16))
            xr3 = [E(nc.sbuf_tensor(f"C_xr{i}", [128, D], F32))
                   for i in range(2)]
            xn3 = [E(nc.sbuf_tensor(f"C_xn{i}", [128, D], F32))
                   for i in range(2)]
            psS = [E(nc.psum_tensor(f"C_psS{i}", [128, 512], F32))
                   for i in range(4)]
            psV = [E(nc.psum_tensor(f"C_psV{i}", [128, 1024], BF16))
                   for i in range(4)]
            psVacc = [p[:, 0:2 * (HD + 1)].bitcast(F32) for p in psV]
            psVz = [p[:, 2 * HD:2 * (HD + 1)].bitcast(F32) for p in psV]
            psVo = [p[:, 0:2 * HD].bitcast(F32) for p in psV]
            psVtp = [p[0:64, 512:640] for p in psV]
            Ofm3 = Ofm[:].rearrange("p (c q) -> p c q", c=8)
            woT3 = woT[:].rearrange("p (c o) -> p c o", c=8)

            tok_wo = pr.dma("sync", "Cwo", woT3,
                            aps["woT_d"].rearrange("(c p) o -> p c o", p=128))
            pr.barrier()

            items = [(qg, h) for qg in range(nqg) for h in range(NHEAD)]
            tok_kh_rd = [0, 0]
            tok_hdma = [None] * len(items)

            def fetch_item(i):
                qg, h = items[i]
                g = i % 2
                wk = [("pe", tok_kh_rd[g])]
                t1 = pr.dma("sync", f"Ckh{g}a", kh2[g][0:64, :],
                            kfm[h * HD:(h + 1) * HD, :], waits=wk)
                t2 = pr.dma("sync", f"Ckh{g}b", kh2[g][64:128, :],
                            kfm[h * HD:(h + 1) * HD, :], waits=wk)
                t3 = pr.dma("sync", f"Cqh{g}a", qh2[g][0:64, :],
                            qfm[h * HD:(h + 1) * HD, qg * QG:(qg + 1) * QG],
                            waits=wk)
                t4 = pr.dma("sync", f"Cqh{g}b", qh2[g][64:128, :],
                            qfm[h * HD:(h + 1) * HD, qg * QG:(qg + 1) * QG],
                            waits=wk)
                tok_hdma[i] = (t1, t2, t3, t4)

            tok_bank_rd = [[] for _ in range(4)]
            tok_psV_rd = [[] for _ in range(4)]
            tok_pv_pb = [0] * 6
            tok_exp_pb = [0] * 6
            tok_tp_rd = []
            tok_otmp_dma = None
            tok_ofm_rd = 0
            tok_xn_done = [0, 0]
            tok_xst = [None, None]
            ofm_w = []

            fetch_item(0)
            for i, (qg, h) in enumerate(items):
                g = i % 2
                if i + 1 < len(items):
                    fetch_item(i + 1)
                dmas = tok_hdma[i]
                tok_smm_last = 0
                tok_pv_last = 0
                pend = []

                def emit_pv(kts):
                    nonlocal tok_pv_last
                    for kt in kts:
                        p4 = kt % 6
                        first = (kt == 0)
                        stop_ = (kt == nkt - 1)
                        evac_w = ([w for q in range(4)
                                   for w in tok_psV_rd[q]] if first else [])

                        def pvm(te, kt=kt, p4=p4, h=h, first=first,
                                stop_=stop_, texp=tok_exp_pb[p4],
                                evac_w=evac_w):
                            pr.tw(te, "se", texp)
                            for s, v in evac_w:
                                pr.tw(te, s, v)
                            last = None
                            for qt in range(4):
                                last = te.matmul(
                                    psVacc[qt],
                                    lhsT=pbuf[p4][:, qt * 128:(qt + 1) * 128],
                                    rhs=v4[:, kt, h, :],
                                    start=first, stop=stop_)
                            return last

                        tok = pr._push("tensor", pvm, "pe", 1, [])
                        tok_pv_pb[p4] = tok
                        tok_pv_last = tok

                for ktp in range(nkt // 2):
                    kt0, kt1 = 2 * ktp, 2 * ktp + 1
                    b0, b1 = kt0 % 4, kt1 % 4

                    def smm(te, kt0=kt0, kt1=kt1, b0=b0, b1=b1, g=g,
                            dmas=dmas, w0=list(tok_bank_rd[b0]),
                            w1=list(tok_bank_rd[b1])):
                        for t in dmas:
                            pr.tw(te, t[0], t[1])
                        for s, v in w0 + w1:
                            pr.tw(te, s, v)
                        te.matmul(psS[b0][:],
                                  lhsT=kh2[g][0:64, kt0 * 128:(kt0 + 1) * 128],
                                  rhs=qh2[g][0:64, :], start=True, stop=True)
                        return te.matmul(
                            psS[b1][:],
                            lhsT=kh2[g][64:128, kt1 * 128:(kt1 + 1) * 128],
                            rhs=qh2[g][64:128, :], start=True, stop=True)

                    tok_smm = pr._push("tensor", smm, "pe", 1, [])
                    tok_smm_last = tok_smm
                    for kt, b in ((kt0, b0), (kt1, b1)):
                        p4 = kt % 6
                        tok_e = pr.op("scalar", "activation", pbuf[p4][:],
                                      psS[b][:], AF.Exp, scale=QSC,
                                      bias=bias_all[:, h:h + 1],
                                      waits=[("pe", tok_smm),
                                             ("pe", tok_pv_pb[p4])])
                        tok_exp_pb[p4] = tok_e
                        tok_bank_rd[b] = [("se", tok_e)]
                    if len(pend) == 2:
                        emit_pv(pend.pop(0))
                    pend.append((kt0, kt1))
                for kts in pend:
                    emit_pv(kts)
                tok_kh_rd[g] = tok_smm_last

                for qt in range(4):
                    pr.op("vector", "reciprocal", zrec[:, qt:qt + 1],
                          psVz[qt], waits=[("pe", tok_pv_last)])
                    tok_nm = pr.op("vector", "tensor_scalar",
                                   obuf[:, qt * HD:(qt + 1) * HD],
                                   psVo[qt], zrec[:, qt:qt + 1],
                                   None, op0=ALU.mult)
                    tok_psV_rd[qt] = [("ve", tok_nm)]
                tok_obuf = pr.cnt["ve"]

                def tpm(te, tok_obuf=tok_obuf, wtp=list(tok_tp_rd)):
                    pr.tw(te, "ve", tok_obuf)
                    for s, v in wtp:
                        pr.tw(te, s, v)
                    last = None
                    for qt in range(4):
                        last = te.transpose(
                            psVtp[qt], obuf[:, qt * HD:(qt + 1) * HD],
                            con["ident_bf_sb"][:])
                    return last

                tok_tp = pr._push("tensor", tpm, "pe", 1, [])
                c = h // 2
                tok_tp_rd = []
                for qt in range(4):
                    if h % 2 == 0:
                        tok_cp = pr.op("vector", "tensor_copy",
                                       Ofm3[0:64, c, qt * 128:(qt + 1) * 128],
                                       psVtp[qt],
                                       waits=[("pe", tok_tp),
                                              ("pe", tok_ofm_rd)])
                        ofm_w.append(("ve", tok_cp))
                    else:
                        tok_cp = pr.op("vector", "tensor_copy",
                                       otmp[:, qt * 128:(qt + 1) * 128],
                                       psVtp[qt],
                                       waits=[("pe", tok_tp), tok_otmp_dma])
                    tok_tp_rd.append(("ve", tok_cp))
                    tok_psV_rd[qt].append(("ve", tok_cp))
                if h % 2 == 1:
                    td = pr.dma("sync", "Cob", Ofm3[64:128, c, :], otmp[:],
                                waits=[("ve", pr.cnt["ve"]),
                                       ("pe", tok_ofm_rd)])
                    tok_otmp_dma = td
                    ofm_w.append(td)

                if h != NHEAD - 1:
                    continue
                for tc in range(QG // 128):
                    g2 = tc % 2
                    if g2 == 0:
                        pA_, pB_ = psS[0], psS[1]
                        wbank = tok_bank_rd[0] + tok_bank_rd[1]
                    else:
                        pA_, pB_ = psS[2], psS[3]
                        wbank = tok_bank_rd[2] + tok_bank_rd[3]
                    tq = qg * (QG // 128) + tc
                    tok_xr = pr.dma("sync", f"Cxr{g2}", xr3[g2][:],
                                    aps["x_in"][tq * 128:(tq + 1) * 128, :],
                                    waits=[("ve", tok_xn_done[g2])])

                    def wmm(te, tc=tc, pA_=pA_, pB_=pB_, wofm=list(ofm_w),
                            wbank=list(wbank), txn=tok_xn_done[tc % 2],
                            tok_wo=tok_wo):
                        pr.tw(te, tok_wo[0], tok_wo[1])
                        for s, v in wofm + wbank:
                            pr.tw(te, s, v)
                        if txn:
                            pr.tw(te, "ve", txn)
                        last = None
                        for cc in range(8):
                            for nh, pp in ((0, pA_), (1, pB_)):
                                last = te.matmul(
                                    pp[:],
                                    lhsT=Ofm3[:, cc, tc * 128:(tc + 1) * 128],
                                    rhs=woT3[:, cc, nh * 512:(nh + 1) * 512],
                                    start=(cc == 0), stop=(cc == 7))
                        return last

                    tok_w = pr._push("tensor", wmm, "pe", 1, [])
                    tok_ofm_rd = tok_w
                    tok_xa = pr.op("vector", "tensor_tensor",
                                   xn3[g2][:, 0:512], pA_[:],
                                   xr3[g2][:, 0:512], op=ALU.add,
                                   waits=[("pe", tok_w), tok_xr,
                                          tok_xst[g2]])
                    tok_xb = pr.op("vector", "tensor_tensor",
                                   xn3[g2][:, 512:1024], pB_[:],
                                   xr3[g2][:, 512:1024], op=ALU.add)
                    tok_xn_done[g2] = tok_xb
                    if g2 == 0:
                        tok_bank_rd[0] = [("ve", tok_xa)]
                        tok_bank_rd[1] = [("ve", tok_xb)]
                    else:
                        tok_bank_rd[2] = [("ve", tok_xa)]
                        tok_bank_rd[3] = [("ve", tok_xb)]
                    tok_xst[g2] = pr.dma(
                        "sync", f"Cxst{g2}",
                        aps["x_out"][tq * 128:(tq + 1) * 128, :],
                        xn3[g2][:], waits=[("ve", tok_xb)])
                ofm_w = []
            pr.barrier()


def emit_phase_D(pr, nc, T, aps, con):
    emit_prep_l1ffn(pr, nc, aps, con)
    nhc = FF1 // 128
    with contextlib.ExitStack() as stP:
        EP = stP.enter_context
        consts_sb(pr, nc, stP, aps, con, need=("ident_bf",))
        xlnT = EP(nc.sbuf_tensor("D_xlnT", [128, 8 * T], BF16))
        with contextlib.ExitStack() as stln:
            emit_ln_pass(pr, stln, nc, T, aps["x_in"], xlnT, con, "Dln")
        xlnT3 = xlnT[:].rearrange("p (c t) -> p c t", c=8)

        with contextlib.ExitStack() as st:
            E = st.enter_context
            wdT = E(nc.sbuf_tensor("D_wdT", [128, nhc * D], BF16))
            wgb = [E(nc.sbuf_tensor(f"D_wg{i}", [128, 8 * 512], BF16))
                   for i in range(2)]
            wub = [E(nc.sbuf_tensor(f"D_wu{i}", [128, 8 * 512], BF16))
                   for i in range(2)]
            sg = E(nc.sbuf_tensor("D_sg", [128, 512], F32))
            hT = E(nc.sbuf_tensor("D_hT", [128, nhc * 512], BF16))
            xr = [E(nc.sbuf_tensor(f"D_xr{i}", [128, D], F32)) for i in range(2)]
            yo = [E(nc.sbuf_tensor(f"D_yo{i}", [128, D], F32)) for i in range(2)]
            psG = [E(nc.psum_tensor(f"D_psG{i}", [128, 512], F32))
                   for i in range(2)]
            psU = [E(nc.psum_tensor(f"D_psU{i}", [128, 512], F32))
                   for i in range(2)]
            psY = [E(nc.psum_tensor(f"D_psY{i}", [128, D], F32))
                   for i in range(2)]
            wdT3 = wdT[:].rearrange("p (hc o) -> p hc o", o=D)
            hT3 = hT[:].rearrange("p (hc t) -> p hc t", t=512)

            tok_wd = pr.dma("sync", "DwdT", wdT3,
                            aps["wdT_d"].rearrange("(hc p) o -> p hc o", p=128))
            tok_w_rd = [0, 0]
            tok_h_ve = 0
            tok_mm2 = 0
            tok_sil = [0, 0]
            tok_xr_rd = [0, 0]
            tok_yev = [0, 0]
            tok_yst = [None, None]
            for nb in range(T // 512):
                for hcb in range(nhc // 4):
                    wpar = hcb % 2
                    wg3 = wgb[wpar][:].rearrange("p (c h) -> p c h", c=8)
                    wu3 = wub[wpar][:].rearrange("p (c h) -> p c h", c=8)
                    tok_wg = pr.dma("gpsimd", f"Dwg{wpar}", wg3,
                                    aps["wgT_d"][:, hcb * 512:(hcb + 1) * 512]
                                    .rearrange("(c p) h -> p c h", p=128),
                                    waits=[("pe", tok_w_rd[wpar])])
                    tok_wu = pr.dma("gpsimd", f"Dwu{wpar}", wu3,
                                    aps["wuT_d"][:, hcb * 512:(hcb + 1) * 512]
                                    .rearrange("(c p) h -> p c h", p=128),
                                    waits=[("pe", tok_w_rd[wpar])])
                    for hc4 in range(4):
                        hc = hcb * 4 + hc4
                        par = hc % 2

                        def gm(te, hc4=hc4, nb=nb, par=par, wg3=wg3, wu3=wu3,
                               tok_wg=tok_wg, tok_wu=tok_wu, tsil=tok_sil[par],
                               thve=tok_h_ve):
                            pr.tw(te, tok_wg[0], tok_wg[1])
                            pr.tw(te, tok_wu[0], tok_wu[1])
                            if tsil:
                                pr.tw(te, "se", tsil)
                                pr.tw(te, "ve", thve)
                            last = None
                            for c in range(8):
                                last = te.matmul(
                                    psG[par][:],
                                    lhsT=wg3[:, c, hc4 * 128:(hc4 + 1) * 128],
                                    rhs=xlnT3[:, c, nb * 512:(nb + 1) * 512],
                                    start=(c == 0), stop=(c == 7))
                            for c in range(8):
                                last = te.matmul(
                                    psU[par][:],
                                    lhsT=wu3[:, c, hc4 * 128:(hc4 + 1) * 128],
                                    rhs=xlnT3[:, c, nb * 512:(nb + 1) * 512],
                                    start=(c == 0), stop=(c == 7))
                            return last

                        tok_g = pr._push("tensor", gm, "pe", 1, [])
                        tok_w_rd[wpar] = tok_g
                        tok_si = pr.op("scalar", "activation", sg[:],
                                       psG[par][:], AF.Silu,
                                       waits=[("pe", tok_g),
                                              ("ve", tok_h_ve)])
                        tok_sil[par] = tok_si
                        tok_h_ve = pr.op("vector", "tensor_tensor",
                                         hT3[:, hc, :], sg[:], psU[par][:],
                                         op=ALU.mult,
                                         waits=[("se", tok_si),
                                                ("pe", tok_mm2)])
                for tc in range(4):
                    ypar = tc % 2
                    tq = nb * 4 + tc
                    tok_xr = pr.dma("sync", f"Dxr{ypar}", xr[ypar][:],
                                    aps["x_in"][tq * 128:(tq + 1) * 128, :],
                                    waits=[("ve", tok_xr_rd[ypar])])

                    def m2(te, tc=tc, ypar=ypar, thve=tok_h_ve, twd=tok_wd,
                           tyev=tok_yev[ypar]):
                        pr.tw(te, "ve", thve)
                        pr.tw(te, twd[0], twd[1])
                        if tyev:
                            pr.tw(te, "ve", tyev)
                        last = None
                        for hc in range(nhc):
                            for nh in range(2):
                                last = te.matmul(
                                    psY[ypar][:, nh * 512:(nh + 1) * 512],
                                    lhsT=hT3[:, hc, tc * 128:(tc + 1) * 128],
                                    rhs=wdT3[:, hc, nh * 512:(nh + 1) * 512],
                                    start=(hc == 0), stop=(hc == nhc - 1))
                        return last

                    tok_m2 = pr._push("tensor", m2, "pe", 1, [])
                    tok_mm2 = tok_m2
                    tok_ye = pr.op("vector", "tensor_tensor", yo[ypar][:],
                                   psY[ypar][:], xr[ypar][:], op=ALU.add,
                                   waits=[("pe", tok_m2), tok_xr,
                                          tok_yst[ypar]])
                    tok_xr_rd[ypar] = tok_ye
                    tok_yev[ypar] = tok_ye
                    tok_yst[ypar] = pr.dma(
                        "sync", f"Dyst{ypar}",
                        aps["x_out"][tq * 128:(tq + 1) * 128, :],
                        yo[ypar][:], waits=[("ve", tok_ye)])
            pr.barrier()


PHASE_INPUTS = {
    "A": ["x_in", "WQ", "WK", "WV", "xq_cos", "xq_sin", "xk_cos", "xk_sin",
          "ident_bf", "ident_f32", "ones_f32", "ones_bf"],
    "B": ["x_in", "W1", "b1", "W2", "b2", "Wr", "br", "ident_bf", "ident_f32",
          "ones_bf", "ltri"],
    "C": ["x_in", "wq", "wk_a", "wk_b", "wv_a", "wv_b", "wo", "ropeC", "ropeS",
          "ident_bf", "ident_f32", "ones_f32", "headmask2"],
    "D": ["x_in", "w_gate", "w_up", "w_down", "ident_bf"],
}

INPUT_SHAPES = {
    "WQ": ([D, D], F32), "WK": ([D, D], F32), "WV": ([D, D], F32),
    "W1": ([NE, D, FF0], F32), "b1": ([NE, FF0], F32),
    "W2": ([NE, FF0, D], F32), "b2": ([NE, D], F32),
    "Wr": ([D, NE], F32), "br": ([NE], F32),
    "wq": ([D, D], F32), "wk_a": ([LATD, D], F32), "wk_b": ([D, LATD], F32),
    "wv_a": ([LATD, D], F32), "wv_b": ([D, LATD], F32), "wo": ([D, D], F32),
    "w_gate": ([FF1, D], I32), "w_up": ([FF1, D], I32),
    "w_down": ([D, FF1], I32),
}


def _phase_scratch(nc, phase, T):
    s = {}
    if phase == "A":
        s["qfm_d"] = nc.dram_tensor("qfm_d", [D, T], BF16, kind="Internal").ap()
        s["vtm_d"] = nc.dram_tensor("vtm_d", [T, D], BF16, kind="Internal").ap()
        s["brow_d"] = nc.dram_tensor("brow_d", [1, T], F32, kind="Internal").ap()
    elif phase == "B":
        s["xln_tm_d"] = nc.dram_tensor("xln_tm_d", [T, D], BF16,
                                       kind="Internal").ap()
        s["w_d"] = nc.dram_tensor("w_d", [T, 1], F32, kind="Internal").ap()
        s["inv_d"] = nc.dram_tensor("inv_d", [NE * min(CAP, T), 1], U32,
                                    kind="Internal").ap()
        s["ys_d"] = nc.dram_tensor("ys_d", [T, D], F32, kind="Internal").ap()
    elif phase == "C":
        for n, shp in (("wqT_d", [D, D]), ("wkaT_d", [D, LATD]),
                       ("wkbT_d", [LATD, D]), ("wvaT_d", [D, LATD]),
                       ("wvbT_d", [LATD, D]), ("woT_d", [D, D]),
                       ("qfm1_d", [D, T]), ("kfm1_d", [D, T])):
            s[n] = nc.dram_tensor(n, shp, BF16, kind="Internal").ap()
        s["nbg_d"] = nc.dram_tensor("nbg_d", [1, NHEAD], F32,
                                    kind="Internal").ap()
    elif phase == "D":
        for n, shp in (("wgT_d", [D, FF1]), ("wuT_d", [D, FF1]),
                       ("wdT_d", [FF1, D])):
            s[n] = nc.dram_tensor(n, shp, BF16, kind="Internal").ap()
    return s


EMITTERS = {"A": emit_phase_A, "B": emit_phase_B, "C": emit_phase_C,
            "D": emit_phase_D}


def build_phases_nc(phases, T):
    nc = bass.Bass("TRN2", target_bir_lowering=False)
    con_np = host_consts(T)
    aps = {}
    needed = []
    for ph in phases:
        for n in PHASE_INPUTS[ph]:
            if n != "x_in" and n not in needed:
                needed.append(n)
    for n in needed:
        if n in INPUT_SHAPES:
            shp, dt = INPUT_SHAPES[n]
        else:
            arr = con_np[n]
            shp, dt = list(arr.shape), CONST_DTYPES[n]
        aps[n] = nc.dram_tensor(n, shp, dt, kind="ExternalInput").ap()
    xs = [nc.dram_tensor("x_in", [T, D], F32, kind="ExternalInput").ap()]
    for i in range(len(phases) - 1):
        xs.append(nc.dram_tensor(f"x_mid{i}", [T, D], F32, kind="Internal").ap())
    xs.append(nc.dram_tensor("x_out", [T, D], F32, kind="ExternalOutput").ap())
    for ph in phases:
        aps.update(_phase_scratch(nc, ph, T))

    with contextlib.ExitStack() as st:
        block = st.enter_context(nc.Block())
        pr = Prog(nc, block, st)
        for i, ph in enumerate(phases):
            con = dict(con_np)
            paps = dict(aps)
            paps["x_in"] = xs[i]
            paps["x_out"] = xs[i + 1]
            EMITTERS[ph](pr, nc, T, paps, con)
        pr.finish()
    return nc, needed


def kernel(**inputs):
    X = np.ascontiguousarray(np.asarray(inputs["L0_attn_X"], dtype=np.float32))
    B, T, _ = X.shape
    con = host_consts(T)
    name_map = {
        "WQ": "L0_attn_W_Q", "WK": "L0_attn_W_K", "WV": "L0_attn_W_V",
        "W1": "L0_ffn_W1", "b1": "L0_ffn_b1", "W2": "L0_ffn_W2",
        "b2": "L0_ffn_b2", "Wr": "L0_ffn_Wr", "br": "L0_ffn_br",
        "wq": "L1_attn_wq", "wk_a": "L1_attn_wk_a", "wk_b": "L1_attn_wk_b",
        "wv_a": "L1_attn_wv_a", "wv_b": "L1_attn_wv_b", "wo": "L1_attn_wo",
        "w_gate": "L1_ffn_w_gate", "w_up": "L1_ffn_w_up",
        "w_down": "L1_ffn_w_down",
    }

    def get_input(n):
        if n in name_map:
            arr = np.asarray(inputs[name_map[n]])
            dt = np.int32 if INPUT_SHAPES[n][1] == I32 else np.float32
            return np.ascontiguousarray(arr.astype(dt, copy=False))
        return con[n]

    nc, needed = build_phases_nc(("A", "B", "C", "D"), T)
    base = {n: get_input(n) for n in needed}
    in_maps = [dict(base, x_in=np.ascontiguousarray(X[c])) for c in range(B)]
    res = run_bass_kernel_spmd(nc, in_maps, core_ids=list(range(B)))
    return np.stack([np.asarray(res.results[c]["x_out"], dtype=np.float32)
                     for c in range(B)], axis=0)


# version 10
# speedup vs baseline: 1.0509x; 1.0509x over previous
import contextlib

import numpy as np
import ml_dtypes

import concourse.bass as bass
import concourse.mybir as mybir
from concourse.bass_utils import run_bass_kernel_spmd

F32 = mybir.dt.float32
BF16 = mybir.dt.bfloat16
I32 = mybir.dt.int32
U32 = mybir.dt.uint32
AF = mybir.ActivationFunctionType
ALU = mybir.AluOpType
AX = mybir.AxisListType
BF = ml_dtypes.bfloat16

D = 1024
NHEAD, HD, DROPE, LATD = 16, 64, 16, 256
FF0, NE = 4096, 8
FF1 = 3072
LNG = float(np.log(np.float32(0.9)))
EPS = 1e-5
QSC = 1.0 / 8.0
DEQ = float(np.float32(1.0) / np.float32(127.0))
NC = 8


def host_consts(T):
    f32 = np.float32
    j = np.arange(512, dtype=f32)
    scale_vec = (2 * j + f32(0.4 * 1024)) / f32(1.4 * 1024)
    power = np.arange(T, dtype=f32) / f32(512.0)
    scale = scale_vec[None, :] ** power[:, None]
    inv_freq = f32(1.0) / (f32(10000.0) ** (j / f32(512.0)))
    pos = np.arange(T, dtype=f32)[:, None]
    sinu = pos * inv_freq[None, :]
    sn, cs = np.sin(sinu).astype(f32), np.cos(sinu).astype(f32)

    def rep2(a):
        return np.ascontiguousarray(np.repeat(a.astype(f32), 2, axis=1))

    sgn = np.tile(np.asarray([-1.0, 1.0], f32), 512)[None, :]
    con = {
        "xq_cos": rep2(cs * scale),
        "xq_sin": np.ascontiguousarray(rep2(sn * scale) * sgn),
        "xk_cos": rep2(cs / scale),
        "xk_sin": np.ascontiguousarray(rep2(sn / scale) * sgn),
    }
    C = np.ones(D, f32)
    S = np.zeros(D, f32)
    inv2 = f32(1.0) / (f32(10000.0) ** (np.arange(0, DROPE, 2, dtype=f32) / f32(HD)))
    for h in range(NHEAD):
        for jj in range(DROPE // 2):
            a = f32(h) * inv2[jj]
            C[h * HD + 2 * jj] = np.cos(a)
            C[h * HD + 2 * jj + 1] = np.cos(a)
            S[h * HD + 2 * jj] = -np.sin(a)
            S[h * HD + 2 * jj + 1] = np.sin(a)
    con["ropeC"] = np.ascontiguousarray(np.broadcast_to(C, (128, D))).astype(f32)
    con["ropeS"] = np.ascontiguousarray(np.broadcast_to(S, (128, D))).astype(f32)
    con["ident_bf"] = np.eye(128).astype(BF)
    con["ident_f32"] = np.eye(128, dtype=f32)
    con["ones_f32"] = np.ones((1, 128), f32)
    con["ones_bf"] = np.ones((1, 512), BF)
    con["headmask2"] = np.stack(
        [(np.arange(128) < 64), (np.arange(128) >= 64)], axis=1).astype(f32)
    con["ltri"] = np.triu(np.ones((128, 128)), 1).astype(BF)
    return con


CONST_DTYPES = {
    "xq_cos": F32, "xq_sin": F32, "xk_cos": F32, "xk_sin": F32,
    "ropeC": F32, "ropeS": F32, "ident_bf": BF16, "ident_f32": F32,
    "ones_f32": F32, "ones_bf": BF16, "headmask2": F32, "ltri": BF16,
}


class Prog:
    ENGS = ("sync", "gpsimd", "tensor", "vector", "scalar")
    CSEM = {"tensor": "pe", "vector": "ve", "scalar": "se", "gpsimd": "gc"}

    def __init__(self, nc, block, stack):
        self.nc, self.block, self.stack = nc, block, stack
        self.sems = {}
        self.cnt = {}
        for k in ("pe", "ve", "se", "gc"):
            self._new_sem(k)
        self.pending = {e: [] for e in self.ENGS}
        self.waited = {e: {} for e in self.ENGS}
        self.bar = {e: [] for e in self.ENGS}

    def _new_sem(self, name):
        self.sems[name] = self.stack.enter_context(self.nc.semaphore(name))
        self.cnt[name] = 0

    def _push(self, eng, fn, sem, inc, waits):
        w = self.bar[eng] + [x for x in waits if x is not None and x[1] > 0]
        for s, v in w:
            assert v <= self.cnt[s], \
                f"wait on future token {s}={v} (cnt {self.cnt[s]})"
        self.bar[eng] = []
        self.cnt[sem] += inc
        tok = self.cnt[sem]
        self.pending[eng].append((w, fn, sem, inc))
        return tok

    def dma(self, eng, slot, out, in_, waits=(), **kw):
        if slot not in self.sems:
            self._new_sem(slot)
        tok = self._push(eng, lambda e: e.dma_start(out=out, in_=in_, **kw),
                         slot, 16, waits)
        return (slot, tok)

    def idma(self, slot, out, out_offset, in_, in_offset, waits=(), **kw):
        if slot not in self.sems:
            self._new_sem(slot)
        tok = self._push(
            "gpsimd",
            lambda e: e.indirect_dma_start(out=out, out_offset=out_offset,
                                           in_=in_, in_offset=in_offset, **kw),
            slot, 16, waits)
        return (slot, tok)

    def op(self, eng, meth, *args, waits=(), chain=True, **kw):
        sem = self.CSEM[eng]
        w = list(waits)
        if chain and eng != "tensor" and self.cnt[sem] > 0:
            w.append((sem, self.cnt[sem]))
        return self._push(eng, lambda e: getattr(e, meth)(*args, **kw), sem, 1, w)

    def mm(self, fn, waits=()):
        return self._push("tensor", fn, "pe", 1, waits)

    def tw(self, te, s, v):
        eng = self._cur
        if v > self.waited[eng].get(s, 0):
            te.wait_ge(self.sems[s], v)
            self.waited[eng][s] = v

    def flush(self):
        for eng in self.ENGS:
            steps = self.pending[eng]
            if not steps:
                continue
            self.pending[eng] = []

            def body(e, steps=steps, eng=eng):
                self._cur = eng
                for w, fn, sem, inc in steps:
                    for s, v in w:
                        if v > self.waited[eng].get(s, 0):
                            e.wait_ge(self.sems[s], v)
                            self.waited[eng][s] = v
                    fn(e).then_inc(self.sems[sem], inc)

            getattr(self.block, eng)(body)

    def barrier(self):
        self.flush()
        snap = [(s, v) for s, v in self.cnt.items() if v > 0]
        for e in self.ENGS:
            self.bar[e] = list(snap)

    def finish(self):
        self.barrier()
        snap = [(s, v) for s, v in self.cnt.items() if v > 0]

        def body(e):
            for s, v in snap:
                if v > self.waited["sync"].get(s, 0):
                    e.wait_ge(self.sems[s], v)
                    self.waited["sync"][s] = v

        self.block.sync(body)


_UNIQ = [0]


def consts_sb(pr, nc, st, aps, con, need):
    loaded = False
    for name in need:
        key = name + "_sb"
        if key in con:
            continue
        src = aps[name]
        _UNIQ[0] += 1
        h = st.enter_context(
            nc.sbuf_tensor(f"{key}_{_UNIQ[0]}", list(src.shape),
                           CONST_DTYPES[name]))
        pr.dma("sync", "cst_" + name, h[:], src)
        con[key] = h
        loaded = True
    if loaded:
        pr.barrier()


def emit_ln_pass(pr, st, nc, T, x_dram, xlnT, con, tag, router=None,
                 xlnT_dram=None, xln_tm_dram=None):
    nqt = T // 128
    E = st.enter_context
    xt = [E(nc.sbuf_tensor(f"{tag}_xt{i}", [128, D], F32)) for i in range(2)]
    bns = E(nc.sbuf_tensor(f"{tag}_bns", [128, 12], F32))
    mv = E(nc.sbuf_tensor(f"{tag}_mv", [128, 2], F32))
    rstd = E(nc.sbuf_tensor(f"{tag}_rstd", [128, 1], F32))
    nmr = E(nc.sbuf_tensor(f"{tag}_nmr", [128, 1], F32))
    tmp1 = E(nc.sbuf_tensor(f"{tag}_tmp1", [128, 1], F32))
    xln = [E(nc.sbuf_tensor(f"{tag}_xln{i}", [128, D], BF16)) for i in range(2)]
    pst = [E(nc.psum_tensor(f"{tag}_pst{i}", [128, D], BF16)) for i in range(2)]
    epsc = E(nc.sbuf_tensor(f"{tag}_eps", [128, 1], F32))
    pr.op("vector", "memset", epsc[:], EPS)
    ident = con["ident_bf_sb"]
    xlnT3 = xlnT[:].rearrange("p (c t) -> p c t", c=8)

    tok_xt_rd = [0, 0]
    tok_ev = [0, 0]
    tok_tp = [0, 0]
    tok_tm = [None, None]
    r = router
    for t in range(nqt):
        b = t % 2
        tok_x = pr.dma("sync", f"{tag}x{b}", xt[b][:],
                       x_dram[t * 128:(t + 1) * 128, :],
                       waits=[("ve", tok_xt_rd[b])])
        pr.op("vector", "bn_stats", bns[:, 0:6], xt[b][:, 0:512],
              waits=[tok_x])
        pr.op("vector", "bn_stats", bns[:, 6:12], xt[b][:, 512:1024])
        pr.op("vector", "bn_aggr", mv[:], bns[:])
        tok_mv = pr.cnt["ve"]
        pr.op("scalar", "activation", tmp1[:], mv[:, 1:2], AF.Ln,
              bias=epsc[:, 0:1], waits=[("ve", tok_mv)])
        tok_rstd = pr.op("scalar", "activation", rstd[:], tmp1[:], AF.Exp,
                         scale=-0.5)
        pr.op("vector", "tensor_scalar", nmr[:], mv[:, 0:1], rstd[:, 0:1],
              -1.0, op0=ALU.mult, op1=ALU.mult, waits=[("se", tok_rstd)])
        tok_xln = pr.op("vector", "tensor_scalar", xln[b][:], xt[b][:],
                        rstd[:, 0:1], nmr[:, 0:1], op0=ALU.mult, op1=ALU.add,
                        waits=[("pe", tok_tp[b]), tok_tm[b]])
        tok_xt_rd[b] = tok_xln
        if xln_tm_dram is not None:
            tok_tm[b] = pr.dma("sync", f"{tag}tm{b}",
                               xln_tm_dram[t * 128:(t + 1) * 128, :],
                               xln[b][:], waits=[("ve", tok_xln)])

        def tp(te, b=b, tok_xln=tok_xln, tev=tok_ev[b]):
            pr.tw(te, "ve", tok_xln)
            if tev:
                pr.tw(te, "ve", tev)
            last = None
            for c in range(8):
                last = te.transpose(pst[b][:, c * 128:(c + 1) * 128],
                                    xln[b][:, c * 128:(c + 1) * 128], ident[:])
            return last

        tok_tp[b] = pr._push("tensor", tp, "pe", 1, [])
        dst = xlnT3[:, :, t * 128:(t + 1) * 128]
        tok_ev[b] = pr.op("vector", "tensor_copy", dst,
                          pst[b][:].rearrange("p (c t) -> p c t", c=8),
                          waits=[("pe", tok_tp[b])])
        if xlnT_dram is not None:
            ddst = xlnT_dram.rearrange("(c p) t -> p c t", p=128)
            pr.dma("sync", f"{tag}st{b}", ddst[:, :, t * 128:(t + 1) * 128],
                   dst, waits=[("ve", tok_ev[b])])
        if r is not None:
            pp = r["psr"][t % 2]

            def rmm(te, t=t, b=b, pp=pp, tev=tok_ev[b],
                    tg=r["tok_gate"][t % 2], twr=r["tok_wr"],
                    tbr=r["tok_br"]):
                pr.tw(te, twr[0], twr[1])
                pr.tw(te, tbr[0], tbr[1])
                pr.tw(te, "ve", tev)
                if tg:
                    pr.tw(te, "ve", tg)
                for c in range(8):
                    te.matmul(pp[:],
                              lhsT=xlnT3[:, c, t * 128:(t + 1) * 128],
                              rhs=r["wr_sb"][:].rearrange(
                                  "p (c e) -> p c e", c=8)[:, c, :],
                              start=(c == 0), stop=False)
                return te.matmul(pp[:], lhsT=con["ones_bf_sb"][:, 0:128],
                                 rhs=r["br_sb"][:], start=False, stop=True)

            tok_rm = pr._push("tensor", rmm, "pe", 1, [])
            pr.op("vector", "tensor_reduce", r["negmx"][:], pp[:], axis=AX.X,
                  op=ALU.max, negate=True, waits=[("pe", tok_rm)])
            pr.op("vector", "tensor_scalar", r["mx"][:], r["negmx"][:], -1.0,
                  None, op0=ALU.mult)
            tok_e = pr.op("scalar", "activation", r["er"][:], pp[:], AF.Exp,
                          bias=r["negmx"][:, 0:1], accum_out=r["z8"][:],
                          waits=[("ve", pr.cnt["ve"])])
            pr.op("vector", "reciprocal", r["g8"][:], r["z8"][:],
                  waits=[("se", tok_e)])
            pr.op("vector", "tensor_scalar", r["sel"][:], pp[:],
                  r["mx"][:, 0:1], None, op0=ALU.is_equal)
            pr.op("vector", "scalar_tensor_tensor", r["mi"][:], r["sel"][:],
                  -1000.0, r["e_bc"][:], op0=ALU.mult, op1=ALU.add,
                  waits=[("gc", r["tok_ebc"])])
            pr.op("vector", "tensor_reduce", r["idxm"][:], r["mi"][:],
                  axis=AX.X, op=ALU.min)
            pr.op("vector", "tensor_scalar", r["idxp"][:], r["idxm"][:],
                  1000.0, None, op0=ALU.add)
            pr.op("vector", "tensor_scalar", r["self_"][:], r["e_bc"][:],
                  r["idxp"][:, 0:1], None, op0=ALU.is_equal)
            pr.op("vector", "tensor_copy",
                  r["sel4"][:, :, t:t + 1],
                  r["self_"][:].rearrange("p (e o) -> p e o", o=1))
            r["tok_gate"][t % 2] = pr.op("vector", "tensor_copy",
                                         r["wtok"][:, t:t + 1], r["g8"][:])
    pr.barrier()


def emit_phase_A(pr, nc, T, aps, con):
    nqt = T // 128
    QB = min(512, T)
    nqb = T // QB
    with contextlib.ExitStack() as stP:
        EP = stP.enter_context
        kfm = EP(nc.sbuf_tensor("A_kfm", [128, 8 * T], BF16))
        consts_sb(pr, nc, stP, aps, con,
                  need=("ident_bf", "ident_f32", "ones_f32", "ones_bf"))
        kfm3 = kfm[:].rearrange("p (c t) -> p c t", c=8)
        qfm_r = aps["qfm_d"].rearrange("(c p) t -> p c t", p=128)

        with contextlib.ExitStack() as st:
            E = st.enter_context
            xlnT = E(nc.sbuf_tensor("A_xlnT", [128, 8 * T], BF16))
            with contextlib.ExitStack() as stln:
                emit_ln_pass(pr, stln, nc, T, aps["x_in"], xlnT, con, "Aln")
            xlnT3 = xlnT[:].rearrange("p (c t) -> p c t", c=8)

            wb = E(nc.sbuf_tensor("A_wb", [128, 8 * D], BF16))
            psq = [E(nc.psum_tensor(f"A_psq{i}", [128, D], F32)) for i in range(2)]
            pstp = [E(nc.psum_tensor(f"A_pstp{i}", [128, D], BF16)) for i in range(2)]
            vbuf = [E(nc.sbuf_tensor(f"A_vb{i}", [128, D], BF16)) for i in range(2)]
            cosb = [E(nc.sbuf_tensor(f"A_cos{i}", [128, D], F32)) for i in range(2)]
            sinb = [E(nc.sbuf_tensor(f"A_sin{i}", [128, D], F32)) for i in range(2)]
            t1b = E(nc.sbuf_tensor("A_t1", [128, D], F32))
            swb = E(nc.sbuf_tensor("A_sw", [128, D], F32))
            qxb = [E(nc.sbuf_tensor(f"A_qx{i}", [128, D], BF16)) for i in range(2)]
            qfb = [E(nc.sbuf_tensor(f"A_qf{i}", [128, D], BF16)) for i in range(2)]
            wb3 = wb[:].rearrange("p (c o) -> p c o", c=8)

            for wname in ("WV", "WK", "WQ"):
                tok_w = pr.dma("gpsimd", "Awb", wb3,
                               aps[wname].rearrange("(c p) o -> p c o", p=128),
                               waits=[("pe", pr.cnt["pe"])])
                tok_rd_ve = [0, 0]
                tok_rd_se = [0, 0]
                tok_st = [None, None]
                tok_tp = [0, 0]
                tok_ev2 = [0, 0]
                pend_tp = None

                def flush_tp(pend, wname=wname):
                    t, b, tok_qx = pend

                    def tpf(te, b=b, tok_qx=tok_qx, tev=tok_ev2[b]):
                        pr.tw(te, "ve", tok_qx)
                        if tev:
                            pr.tw(te, "ve", tev)
                        last = None
                        for c in range(8):
                            last = te.transpose(
                                pstp[b][:, c * 128:(c + 1) * 128],
                                qxb[b][:, c * 128:(c + 1) * 128],
                                con["ident_bf_sb"][:])
                        return last

                    tok_tp[b] = pr._push("tensor", tpf, "pe", 1, [])
                    ps3 = pstp[b][:].rearrange("p (c t) -> p c t", c=8)
                    if wname == "WK":
                        tok_ev2[b] = pr.op(
                            "vector", "tensor_copy",
                            kfm3[:, :, t * 128:(t + 1) * 128], ps3,
                            waits=[("pe", tok_tp[b])])
                    else:
                        qf3 = qfb[b][:].rearrange("p (c t) -> p c t", c=8)
                        tok_ev2[b] = pr.op("vector", "tensor_copy", qf3, ps3,
                                           waits=[("pe", tok_tp[b]),
                                                  tok_st[b]])
                        tok_st[b] = pr.dma(
                            "sync", f"Aqst{b}",
                            qfm_r[:, :, t * 128:(t + 1) * 128], qf3,
                            waits=[("ve", tok_ev2[b])])

                for t in range(nqt):
                    b = t % 2

                    def qmm(te, t=t, b=b, tok_w=tok_w, trv=tok_rd_ve[b],
                            trs=tok_rd_se[b]):
                        pr.tw(te, tok_w[0], tok_w[1])
                        if trv:
                            pr.tw(te, "ve", trv)
                        if trs:
                            pr.tw(te, "se", trs)
                        last = None
                        for nh in range(2):
                            for c in range(8):
                                last = te.matmul(
                                    psq[b][:, nh * 512:(nh + 1) * 512],
                                    lhsT=xlnT3[:, c, t * 128:(t + 1) * 128],
                                    rhs=wb3[:, c, nh * 512:(nh + 1) * 512],
                                    start=(c == 0), stop=(c == 7))
                        return last

                    tok_mm = pr._push("tensor", qmm, "pe", 1, [])
                    if wname == "WV":
                        tok_ev = pr.op("vector", "tensor_copy", vbuf[b][:],
                                       psq[b][:],
                                       waits=[("pe", tok_mm), tok_st[b]])
                        tok_rd_ve[b] = tok_ev
                        tok_st[b] = pr.dma(
                            "sync", f"Avst{b}",
                            aps["vtm_d"][t * 128:(t + 1) * 128, :],
                            vbuf[b][:], waits=[("ve", tok_ev)])
                    else:
                        cn, sn = (("xk_cos", "xk_sin") if wname == "WK"
                                  else ("xq_cos", "xq_sin"))
                        tok_c = pr.dma("sync", f"Acos{b}", cosb[b][:],
                                       aps[cn][t * 128:(t + 1) * 128, :],
                                       waits=[("ve", tok_rd_ve[b])])
                        tok_s = pr.dma("sync", f"Asin{b}", sinb[b][:],
                                       aps[sn][t * 128:(t + 1) * 128, :],
                                       waits=[("ve", tok_rd_ve[b])])
                        pr.op("vector", "tensor_tensor", t1b[:], psq[b][:],
                              cosb[b][:], op=ALU.mult,
                              waits=[("pe", tok_mm), tok_c])
                        vps = psq[b][:].rearrange("p (n two) -> p n two", two=2)
                        vsw = swb[:].rearrange("p (n two) -> p n two", two=2)
                        pr.op("scalar", "copy", vsw[:, :, 0:1], vps[:, :, 1:2],
                              waits=[("pe", tok_mm), ("ve", pr.cnt["ve"])])
                        tok_sw = pr.op("scalar", "copy", vsw[:, :, 1:2],
                                       vps[:, :, 0:1])
                        tok_rd_se[b] = tok_sw
                        pr.op("vector", "tensor_tensor", swb[:], swb[:],
                              sinb[b][:], op=ALU.mult,
                              waits=[("se", tok_sw), tok_s])
                        tok_qx = pr.op("vector", "tensor_tensor", qxb[b][:],
                                       t1b[:], swb[:], op=ALU.add,
                                       waits=[("pe", tok_tp[b])])
                        tok_rd_ve[b] = tok_qx
                        if pend_tp is not None:
                            flush_tp(pend_tp)
                        pend_tp = (t, b, tok_qx)
                if pend_tp is not None:
                    flush_tp(pend_tp)
                    pend_tp = None
                pr.barrier()

        negBc = stP.enter_context(nc.sbuf_tensor("A_negBc", [128, nqt], F32))
        with contextlib.ExitStack() as st:
            E = st.enter_context
            qcol = [E(nc.sbuf_tensor(f"A_qc{i}", [128, 8 * 128], BF16))
                    for i in range(2)]
            psA = E(nc.psum_tensor("A_psA", [128, T], F32))
            esc = E(nc.sbuf_tensor("A_esc", [128, T], F32))
            negmx = E(nc.sbuf_tensor("A_negmx", [128, 1], F32))
            zt = E(nc.sbuf_tensor("A_zt", [128, 1], F32))
            lnz = E(nc.sbuf_tensor("A_lnz", [128, 1], F32))
            tok_exp = 0
            for t in range(nqt):
                b = t % 2
                tok_q = pr.dma("sync", f"Aqc{b}", qcol[b][:],
                               qfm_r[:, :, t * 128:(t + 1) * 128],
                               waits=[("pe", pr.cnt["pe"])])
                qc3 = qcol[b][:].rearrange("p (c t) -> p c t", c=8)

                def smm(te, tq=tok_q, qc3=qc3, texp=tok_exp, tve=pr.cnt["ve"]):
                    pr.tw(te, tq[0], tq[1])
                    if texp:
                        pr.tw(te, "se", texp)
                    if tve:
                        pr.tw(te, "ve", tve)
                    last = None
                    for kb in range(T // 512):
                        for c in range(8):
                            last = te.matmul(
                                psA[:, kb * 512:(kb + 1) * 512],
                                lhsT=qc3[:, c, :],
                                rhs=kfm3[:, c, kb * 512:(kb + 1) * 512],
                                start=(c == 0), stop=(c == 7))
                    return last

                tok_mm = pr._push("tensor", smm, "pe", 1, [])
                pr.op("vector", "tensor_reduce", negmx[:], psA[:], axis=AX.X,
                      op=ALU.max, negate=True, waits=[("pe", tok_mm)])
                tok_exp = pr.op("scalar", "activation", esc[:], psA[:], AF.Exp,
                                bias=negmx[:, 0:1], accum_out=zt[:],
                                waits=[("ve", pr.cnt["ve"])])
                pr.op("scalar", "activation", lnz[:], zt[:], AF.Ln)
                pr.op("vector", "tensor_tensor", negBc[:, t:t + 1], negmx[:],
                      lnz[:], op=ALU.subtract, waits=[("se", pr.cnt["se"])])
            pr.barrier()
        with contextlib.ExitStack() as st:
            E = st.enter_context
            ps32 = E(nc.psum_tensor("A_ps32", [nqt, 128], F32))
            nb32 = E(nc.sbuf_tensor("A_nb32", [nqt, 128], F32))
            tok_tp = pr.mm(lambda te: te.transpose(ps32[:], negBc[:],
                                                   con["ident_f32_sb"][:]))
            tok_cp = pr.op("vector", "tensor_copy", nb32[:], ps32[:],
                           waits=[("pe", tok_tp)])
            pr.dma("sync", "Abrow",
                   aps["brow_d"][0, :].rearrange("(a b) -> a b", a=nqt),
                   nb32[:], waits=[("ve", tok_cp)])
            pr.barrier()

        with contextlib.ExitStack() as st:
            E = st.enter_context
            nkt = T // 128
            vtm = E(nc.sbuf_tensor("A_vtm", [128, nkt * D], BF16))
            pD = E(nc.sbuf_tensor("A_pD", [128, nkt * QB], BF16))
            qqb = E(nc.sbuf_tensor("A_qqb", [128, 8 * QB], BF16))
            mt = E(nc.sbuf_tensor("A_mt", [128, QB], F32))
            dm = E(nc.sbuf_tensor("A_dm", [128, QB], F32))
            nbhi = E(nc.sbuf_tensor("A_nbhi", [1, QB], BF16))
            nbhf = E(nc.sbuf_tensor("A_nbhf", [1, QB], F32))
            nblo = E(nc.sbuf_tensor("A_nblo", [1, QB], BF16))
            ssb = [E(nc.sbuf_tensor(f"A_ssb{i}", [128, QB], F32)) for i in range(2)]
            xres = [E(nc.sbuf_tensor(f"A_xr{i}", [128, D], F32)) for i in range(2)]
            xnew = [E(nc.sbuf_tensor(f"A_xn{i}", [128, D], F32)) for i in range(2)]
            psT = [E(nc.psum_tensor(f"A_psT{i}", [128, QB], F32)) for i in range(2)]
            psO = [E(nc.psum_tensor(f"A_psO{i}", [128, D], F32)) for i in range(2)]
            nbq = E(nc.sbuf_tensor("A_nbq", [1, QB], F32))
            vt3 = vtm[:].rearrange("p (k o) -> p k o", o=D)
            pD3 = pD[:].rearrange("p (k q) -> p k q", q=QB)
            qq3 = qqb[:].rearrange("p (c t) -> p c t", c=8)

            tok_v = pr.dma("sync", "Avtm", vt3,
                           aps["vtm_d"].rearrange("(a p) o -> p a o", p=128))
            tok_pv_last = 0
            tok_ssb_done = {0: 0, 1: 0}
            tok_xn_done = {0: 0, 1: 0}
            tok_xst = {0: None, 1: None}
            for qb in range(nqb):
                tok_qq = pr.dma("sync", "Aqqb", qqb[:].rearrange(
                    "p (c t) -> p c t", c=8),
                    qfm_r[:, :, qb * QB:(qb + 1) * QB],
                    waits=[("pe", tok_pv_last)])
                tok_nb = pr.dma("sync", "Anbq", nbq[:],
                                aps["brow_d"][0:1, qb * QB:(qb + 1) * QB],
                                waits=[("pe", tok_pv_last)])
                pr.op("vector", "tensor_copy", nbhi[:], nbq[:],
                      waits=[tok_nb, ("pe", tok_pv_last)])
                pr.op("vector", "tensor_copy", nbhf[:], nbhi[:])
                tok_nlo = pr.op("vector", "tensor_tensor", nblo[:], nbq[:],
                                nbhf[:], op=ALU.subtract)
                tok_exp = {0: 0, 1: 0}
                for kt in range(nkt):
                    b = kt % 2
                    tok_io = pr.op("gpsimd", "iota", mt[:],
                                   pattern=[[1, QB]],
                                   base=qb * QB - kt * 128,
                                   channel_multiplier=-1,
                                   allow_small_or_imprecise_dtypes=True,
                                   waits=[("ve", pr.cnt["ve"])])
                    tok_ab = pr.op("vector", "tensor_scalar",
                                   dm[:].bitcast(I32), mt[:].bitcast(I32),
                                   0x7FFFFFFF, None, op0=ALU.bitwise_and,
                                   waits=[("gc", tok_io)])

                    def stmm(te, kt=kt, b=b, tqq=tok_qq, tsd=tok_ssb_done[b],
                             tnlo=tok_nlo):
                        pr.tw(te, tqq[0], tqq[1])
                        pr.tw(te, "ve", tnlo)
                        if tsd:
                            pr.tw(te, "ve", tsd)
                        last = None
                        for nh in range(QB // 512):
                            for c in range(8):
                                te.matmul(
                                    psT[b][:, nh * 512:(nh + 1) * 512],
                                    lhsT=kfm3[:, c, kt * 128:(kt + 1) * 128],
                                    rhs=qq3[:, c, nh * 512:(nh + 1) * 512],
                                    start=(c == 0), stop=False)
                            te.matmul(psT[b][:, nh * 512:(nh + 1) * 512],
                                      lhsT=con["ones_bf_sb"][:, 0:128],
                                      rhs=nbhi[:, nh * 512:(nh + 1) * 512],
                                      start=False, stop=False)
                            last = te.matmul(
                                psT[b][:, nh * 512:(nh + 1) * 512],
                                lhsT=con["ones_bf_sb"][:, 0:128],
                                rhs=nblo[:, nh * 512:(nh + 1) * 512],
                                start=False, stop=True)
                        return last

                    tok_stm = pr._push("tensor", stmm, "pe", 1, [])
                    tok_ssb = pr.op("vector", "scalar_tensor_tensor",
                                    ssb[b][:], dm[:], LNG, psT[b][:],
                                    op0=ALU.mult, op1=ALU.add,
                                    waits=[("pe", tok_stm),
                                           ("se", tok_exp[b])])
                    tok_ssb_done[b] = tok_ssb
                    tok_exp[b] = pr.op("scalar", "activation", pD3[:, kt, :],
                                       ssb[b][:], AF.Exp,
                                       waits=[("ve", tok_ssb),
                                              ("pe", tok_pv_last)])
                for qt in range(QB // 128):
                    g = qt % 2
                    tq = qb * (QB // 128) + qt
                    tok_xr = pr.dma("sync", f"Axr{g}", xres[g][:],
                                    aps["x_in"][tq * 128:(tq + 1) * 128, :],
                                    waits=[("ve", tok_xn_done[g])])

                    def pvmm(te, qt=qt, g=g, texp=max(tok_exp.values()),
                             txn=tok_xn_done[g]):
                        pr.tw(te, "se", texp)
                        if txn:
                            pr.tw(te, "ve", txn)
                        last = None
                        for kt in range(nkt):
                            for nh in range(2):
                                last = te.matmul(
                                    psO[g][:, nh * 512:(nh + 1) * 512],
                                    lhsT=pD3[:, kt, qt * 128:(qt + 1) * 128],
                                    rhs=vt3[:, kt, nh * 512:(nh + 1) * 512],
                                    start=(kt == 0), stop=(kt == nkt - 1))
                        return last

                    tok_pv = pr._push("tensor", pvmm, "pe", 1,
                                      [tok_v, tok_xr])
                    tok_pv_last = tok_pv
                    tok_xn = pr.op("vector", "tensor_tensor", xnew[g][:],
                                   psO[g][:], xres[g][:], op=ALU.add,
                                   waits=[("pe", tok_pv), tok_xr, tok_xst[g]])
                    tok_xn_done[g] = tok_xn
                    tok_xst[g] = pr.dma("sync", f"Axst{g}",
                                        aps["x_out"][tq * 128:(tq + 1) * 128, :],
                                        xnew[g][:], waits=[("ve", tok_xn)])
            pr.barrier()


CAP = 768
BIGIDX = 1 << 20


def emit_phase_B(pr, nc, T, aps, con):
    from concourse.bass import IndirectOffsetOnAxis
    nqt = T // 128
    cap = min(CAP, T)
    ncap = NE * cap
    rb_cap = nc.gpsimd.alloc_register()
    rb_tok = nc.gpsimd.alloc_register()
    pr.op("gpsimd", "reg_mov", rb_cap, ncap - 1)
    pr.op("gpsimd", "reg_mov", rb_tok, T - 1)
    with contextlib.ExitStack() as stP:
        EP = stP.enter_context
        consts_sb(pr, nc, stP, aps, con,
                  need=("ident_bf", "ident_f32", "ones_bf", "ltri"))
        sel_all = EP(nc.sbuf_tensor("B_sel_all", [128, NE * nqt], F32))
        wtok = EP(nc.sbuf_tensor("B_wtok", [128, nqt], F32))
        ivx = EP(nc.sbuf_tensor("B_ivx", [128, NE * (cap // 128)], U32))
        sel4 = sel_all[:].rearrange("p (e t) -> p e t", e=NE)

        with contextlib.ExitStack() as st:
            E = st.enter_context
            xlnT = E(nc.sbuf_tensor("B_xlnT", [128, 8 * T], BF16))
            r = dict(
                wr_sb=E(nc.sbuf_tensor("B_wr", [128, 8 * NE], BF16)),
                br_sb=E(nc.sbuf_tensor("B_br", [1, NE], BF16)),
                psr=[E(nc.psum_tensor(f"B_psr{i}", [128, NE], F32))
                     for i in range(2)],
                negmx=E(nc.sbuf_tensor("B_ngm", [128, 1], F32)),
                mx=E(nc.sbuf_tensor("B_mx", [128, 1], F32)),
                er=E(nc.sbuf_tensor("B_er", [128, NE], F32)),
                z8=E(nc.sbuf_tensor("B_z8", [128, 1], F32)),
                g8=E(nc.sbuf_tensor("B_g8", [128, 1], F32)),
                sel=E(nc.sbuf_tensor("B_sel", [128, NE], F32)),
                mi=E(nc.sbuf_tensor("B_mi", [128, NE], F32)),
                idxm=E(nc.sbuf_tensor("B_idxm", [128, 1], F32)),
                idxp=E(nc.sbuf_tensor("B_idxp", [128, 1], F32)),
                self_=E(nc.sbuf_tensor("B_self", [128, NE], F32)),
                e_bc=E(nc.sbuf_tensor("B_ebc", [128, NE], F32)),
                sel4=sel4, wtok=wtok,
                tok_gate=[0, 0],
            )
            r["tok_wr"] = pr.dma(
                "gpsimd", "Bwr",
                r["wr_sb"][:].rearrange("p (c e) -> p c e", c=8),
                aps["Wr"].rearrange("(c p) e -> p c e", p=128))
            r["tok_br"] = pr.dma("gpsimd", "Bbr", r["br_sb"][:],
                                 aps["br"].rearrange("(o e) -> o e", o=1))
            r["tok_ebc"] = pr.op("gpsimd", "iota", r["e_bc"][:],
                                 pattern=[[1, NE]], base=0,
                                 channel_multiplier=0,
                                 allow_small_or_imprecise_dtypes=True)
            emit_ln_pass(pr, st, nc, T, aps["x_in"], xlnT, con, "Bln",
                         router=r, xln_tm_dram=aps["xln_tm_d"])

        with contextlib.ExitStack() as st:
            E = st.enter_context
            pexA = E(nc.sbuf_tensor("B_pexA", [128, NE * nqt], F32))
            pexB = E(nc.sbuf_tensor("B_pexB", [128, NE * nqt], F32))
            cnt = E(nc.sbuf_tensor("B_cnt", [128, NE], F32))
            cntb = E(nc.sbuf_tensor("B_cntb", [128, NE], BF16))
            base = E(nc.sbuf_tensor("B_base", [128, NE], F32))
            ecap = E(nc.sbuf_tensor("B_ecap", [128, NE], F32))
            S1 = E(nc.sbuf_tensor("B_S1", [128, NE], F32))
            M2 = E(nc.sbuf_tensor("B_M2", [128, nqt * NE], F32))
            RK = E(nc.sbuf_tensor("B_RK", [128, nqt * NE], F32))
            slots = E(nc.sbuf_tensor("B_slots", [128, nqt], F32))
            rankt = E(nc.sbuf_tensor("B_rank", [128, nqt], F32))
            pen = E(nc.sbuf_tensor("B_pen", [128, nqt], F32))
            slots_u = E(nc.sbuf_tensor("B_slots_u", [128, nqt], U32))
            tki = E(nc.sbuf_tensor("B_tki", [128, nqt], F32))
            tki_u = E(nc.sbuf_tensor("B_tki_u", [128, nqt], U32))
            fillb = E(nc.sbuf_tensor("B_fill", [128, ncap // 128], U32))
            wt32 = E(nc.sbuf_tensor("B_wt32", [nqt, 128], F32))
            zeroc = E(nc.sbuf_tensor("B_zero", [128, D], F32))
            psb = E(nc.psum_tensor("B_psb", [128, 128], F32))
            pA = pexA[:].rearrange("p (e t) -> p e t", e=NE)
            pB = pexB[:].rearrange("p (e t) -> p e t", e=NE)
            M2v = M2[:].rearrange("p (t e) -> p t e", e=NE)
            RKv = RK[:].rearrange("p (t e) -> p t e", e=NE)

            pr.op("vector", "memset", pexA[:], 0.0)
            pr.op("vector", "tensor_copy", pA[:, :, 1:nqt],
                  sel4[:, :, 0:nqt - 1])
            cur, nxt = (pexA, pA), (pexB, pB)
            s = 1
            while s < nqt:
                pr.op("vector", "tensor_copy", nxt[1][:, :, 0:s],
                      cur[1][:, :, 0:s])
                pr.op("vector", "tensor_tensor", nxt[1][:, :, s:nqt],
                      cur[1][:, :, s:nqt], cur[1][:, :, 0:nqt - s],
                      op=ALU.add)
                cur, nxt = nxt, cur
                s *= 2
            pex = cur[1]
            pr.op("vector", "tensor_reduce", cnt[:], sel4, axis=AX.X,
                  op=ALU.add)
            pr.op("vector", "tensor_copy", cntb[:], cnt[:])
            tok_cnt = pr.cnt["ve"]
            tok_bm = pr.mm(lambda te: te.matmul(psb[:, 0:NE],
                                                lhsT=con["ltri_sb"][:],
                                                rhs=cntb[:], start=True,
                                                stop=True),
                           waits=[("ve", tok_cnt)])
            pr.op("vector", "tensor_copy", base[:], psb[:, 0:NE],
                  waits=[("pe", tok_bm)])
            pr.op("gpsimd", "iota", ecap[:], pattern=[[cap, NE]], base=0,
                  channel_multiplier=0, allow_small_or_imprecise_dtypes=True)
            pr.op("vector", "tensor_tensor", S1[:], base[:], ecap[:],
                  op=ALU.add, waits=[("gc", pr.cnt["gc"])])
            for e in range(NE):
                pr.op("vector", "scalar_tensor_tensor", M2v[:, :, e],
                      pex[:, e, :], S1[:, e:e + 1], sel4[:, e, :],
                      op0=ALU.add, op1=ALU.mult)
                pr.op("vector", "scalar_tensor_tensor", RKv[:, :, e],
                      pex[:, e, :], base[:, e:e + 1], sel4[:, e, :],
                      op0=ALU.add, op1=ALU.mult)
            pr.op("vector", "tensor_reduce", slots[:], M2v, axis=AX.X,
                  op=ALU.add)
            pr.op("vector", "tensor_reduce", rankt[:], RKv, axis=AX.X,
                  op=ALU.add)
            pr.op("vector", "tensor_scalar", pen[:], rankt[:], cap - 0.5,
                  None, op0=ALU.is_ge)
            pr.op("vector", "scalar_tensor_tensor", slots[:], pen[:],
                  float(BIGIDX), slots[:], op0=ALU.mult, op1=ALU.add)
            pr.op("vector", "tensor_copy", slots_u[:], slots[:])
            pr.op("gpsimd", "iota", tki[:], pattern=[[128, nqt]], base=0,
                  channel_multiplier=1, allow_small_or_imprecise_dtypes=True)
            tok_rt = pr.op("vector", "tensor_copy", tki_u[:], tki[:],
                           waits=[("gc", pr.cnt["gc"])])
            tok_fl = pr.op("vector", "memset", fillb[:], float(BIGIDX))
            tok_fd = pr.dma("sync", "Binv", 
                            aps["inv_d"][:, 0].rearrange(
                                "(p o) -> p o", o=ncap // 128),
                            fillb[:], waits=[("ve", tok_fl)])
            tok_sc = None
            for t in range(nqt):
                tok_sc = pr.idma(
                    "Bscat", aps["inv_d"],
                    IndirectOffsetOnAxis(ap=slots_u[:, t:t + 1], axis=0),
                    tki_u[:, t:t + 1], None,
                    bounds_check=rb_cap, oob_is_err=False,
                    waits=[("ve", tok_rt), tok_fd])
            tok_wt = pr.mm(lambda te: te.transpose(
                psb[0:nqt, 0:128], wtok[:], con["ident_f32_sb"][:]),
                waits=[("ve", pr.cnt["ve"]), ("pe", tok_bm)])
            tok_wc = pr.op("vector", "tensor_copy", wt32[:],
                           psb[0:nqt, 0:128], waits=[("pe", tok_wt)])
            pr.dma("sync", "Bwd",
                   aps["w_d"][:, 0].rearrange("(a b) -> a b", a=nqt),
                   wt32[:], waits=[("ve", tok_wc)])
            tok_z = pr.op("vector", "memset", zeroc[:], 0.0)
            for t in range(nqt):
                pr.dma("sync", "Bysz",
                       aps["ys_d"][t * 128:(t + 1) * 128, :], zeroc[:],
                       waits=[("ve", tok_z)])
            pr.dma("gpsimd", "Bivx", ivx[:],
                   aps["inv_d"][:, 0].rearrange("(c p) -> p c", p=128),
                   waits=[tok_sc], allow_slow_non_contiguous=True)
            pr.barrier()

        with contextlib.ExitStack() as st:
            E = st.enter_context
            W2sb = E(nc.sbuf_tensor("B_W2", [128, 32 * D], BF16))
            hT = E(nc.sbuf_tensor("B_hT", [128, 32 * cap], BF16))
            xgT = E(nc.sbuf_tensor("B_xgT", [128, 8 * cap], BF16))
            xg = [E(nc.sbuf_tensor(f"B_xg{i}", [128, D], BF16))
                  for i in range(2)]
            w1b = [E(nc.sbuf_tensor(f"B_w1b{i}", [128, 8 * 512], BF16))
                   for i in range(2)]
            b1sq = E(nc.sbuf_tensor("B_b1sq", [32, 128], F32))
            b1col = E(nc.sbuf_tensor("B_b1c", [128, 32], F32))
            b2row = E(nc.sbuf_tensor("B_b2r", [1, D], BF16))
            gw = E(nc.sbuf_tensor("B_gw", [128, NE * (cap // 128)], F32))
            yo = [E(nc.sbuf_tensor(f"B_yo{i}", [128, D], F32))
                  for i in range(2)]
            psH = [E(nc.psum_tensor(f"B_psH{i}", [128, cap], F32))
                   for i in range(2)]
            psY = E(nc.psum_tensor("B_psY", [128, D], F32))
            psX = [E(nc.psum_tensor(f"B_psX{i}", [128, D], BF16))
                   for i in range(2)]
            W2r = W2sb[:].rearrange("p (hc o) -> p hc o", o=D)
            hTr = hT[:].rearrange("p (hc t) -> p hc t", t=cap)
            xgT3 = xgT[:].rearrange("p (c t) -> p c t", c=8)
            nhc = FF0 // 128
            njt = cap // 128

            pr.op("vector", "memset", xg[0][:], 0.0)
            pr.op("vector", "memset", xg[1][:], 0.0)
            pr.barrier()
            tok_mm2_last = 0
            tok_mm1_last = 0
            tok_gel = [0, 0]
            tok_w1_rd = [0, 0]
            tok_yst = [None, None]
            tok_yrd = [0, 0]
            tok_b1_rd = 0
            tok_gw_rd = 0
            tok_xg_rd = [0, 0]
            tok_xgT_wr = 0
            for e in range(NE):
                tok_w2 = pr.dma("gpsimd", "BW2", W2r,
                                aps["W2"][e].rearrange("(hc p) o -> p hc o",
                                                       p=128),
                                waits=[("pe", tok_mm2_last)])
                tok_b2 = pr.dma("gpsimd", "Bb2", b2row[:],
                                aps["b2"][e].rearrange("(o d) -> o d", o=1),
                                waits=[("pe", tok_mm2_last)])
                tok_b1s = pr.dma("sync", "Bb1s", b1sq[:],
                                 aps["b1"][e].rearrange("(a b) -> a b", a=32),
                                 waits=[("pe", tok_b1_rd)])
                tok_b1t = pr.mm(lambda te: te.transpose(
                    psY[:, 0:32], b1sq[:], con["ident_f32_sb"][0:32, 0:32]),
                    waits=[tok_b1s, ("ve", tok_yrd[0]), ("ve", tok_yrd[1])])
                tok_b1_rd = tok_b1t
                tok_b1c = pr.op("vector", "tensor_copy", b1col[:],
                                psY[:, 0:32], waits=[("pe", tok_b1t)])
                tok_gws = None
                for j in range(njt):
                    tok_gws = pr.idma(
                        "Bgw", gw[:, e * njt + j:e * njt + j + 1], None,
                        aps["w_d"],
                        IndirectOffsetOnAxis(
                            ap=ivx[:, e * njt + j:e * njt + j + 1], axis=0),
                        bounds_check=rb_tok, oob_is_err=False,
                        waits=[("ve", tok_gw_rd)])
                tok_tpx = 0
                tok_ev = 0
                for j in range(njt):
                    b = j % 2
                    tok_g = pr.idma(
                        "Bxg" + str(b), xg[b][:], None,
                        aps["xln_tm_d"],
                        IndirectOffsetOnAxis(
                            ap=ivx[:, e * njt + j:e * njt + j + 1], axis=0),
                        bounds_check=rb_tok, oob_is_err=False,
                        waits=[("pe", tok_xg_rd[b])])

                    def tpx(te, b=b, tok_g=tok_g, tev=tok_ev,
                            tm1=tok_mm1_last):
                        pr.tw(te, tok_g[0], tok_g[1])
                        if tev:
                            pr.tw(te, "ve", tev)
                        if tm1:
                            pr.tw(te, "pe", tm1)
                        last = None
                        for c in range(8):
                            last = te.transpose(
                                psX[b][:, c * 128:(c + 1) * 128],
                                xg[b][:, c * 128:(c + 1) * 128],
                                con["ident_bf_sb"][:])
                        return last

                    tok_tpx = pr._push("tensor", tpx, "pe", 1, [])
                    tok_xg_rd[b] = tok_tpx
                    tok_ev = pr.op("vector", "tensor_copy",
                                   xgT3[:, :, j * 128:(j + 1) * 128],
                                   psX[b][:].rearrange("p (c t) -> p c t",
                                                       c=8),
                                   waits=[("pe", tok_tpx)])
                tok_xgT_wr = tok_ev
                for hcb in range(nhc // 4):
                    wpar = hcb % 2
                    w13 = w1b[wpar][:].rearrange("p (c h) -> p c h", c=8)
                    tok_w1 = pr.dma(
                        "gpsimd", f"Bw1{wpar}", w13,
                        aps["W1"][e][:, hcb * 512:(hcb + 1) * 512]
                        .rearrange("(c p) h -> p c h", p=128),
                        waits=[("pe", tok_w1_rd[wpar])])
                    for hc4 in range(4):
                        hc = hcb * 4 + hc4
                        hpar = hc % 2

                        def m1(te, hc4=hc4, w13=w13, hpar=hpar,
                               tok_w1=tok_w1, tgel=tok_gel[hpar],
                               tb1=tok_b1c, txg=tok_xgT_wr):
                            pr.tw(te, tok_w1[0], tok_w1[1])
                            pr.tw(te, "ve", max(tb1, txg))
                            if tgel:
                                pr.tw(te, "se", tgel)
                            last = None
                            for lo in range(0, cap, 512):
                                hi = min(lo + 512, cap)
                                for c in range(8):
                                    last = te.matmul(
                                        psH[hpar][:, lo:hi],
                                        lhsT=w13[:, c,
                                                 hc4 * 128:(hc4 + 1) * 128],
                                        rhs=xgT3[:, c, lo:hi],
                                        start=(c == 0), stop=(c == 7))
                            return last

                        tok_m1 = pr._push("tensor", m1, "pe", 1, [])
                        tok_w1_rd[wpar] = tok_m1
                        tok_mm1_last = tok_m1
                        tok_gel[hpar] = pr.op(
                            "scalar", "activation", hTr[:, hc, :],
                            psH[hpar][:], AF.Gelu,
                            bias=b1col[:, hc:hc + 1],
                            waits=[("pe", tok_m1), ("pe", tok_mm2_last)])
                for tc in range(njt):
                    ypar = tc % 2

                    def m2(te, tc=tc, tok_w2=tok_w2, tok_b2=tok_b2,
                           tgel=pr.cnt["se"], tyrd=max(tok_yrd)):
                        pr.tw(te, tok_w2[0], tok_w2[1])
                        pr.tw(te, tok_b2[0], tok_b2[1])
                        pr.tw(te, "se", tgel)
                        if tyrd:
                            pr.tw(te, "ve", tyrd)
                        last = None
                        for hc in range(nhc):
                            for nh in range(2):
                                last = te.matmul(
                                    psY[:, nh * 512:(nh + 1) * 512],
                                    lhsT=hTr[:, hc, tc * 128:(tc + 1) * 128],
                                    rhs=W2r[:, hc, nh * 512:(nh + 1) * 512],
                                    start=(hc == 0), stop=False)
                        for nh in range(2):
                            last = te.matmul(
                                psY[:, nh * 512:(nh + 1) * 512],
                                lhsT=con["ones_bf_sb"][:, 0:128],
                                rhs=b2row[:, nh * 512:(nh + 1) * 512],
                                start=False, stop=True)
                        return last

                    tok_m2 = pr._push("tensor", m2, "pe", 1, [])
                    tok_mm2_last = tok_m2
                    tok_ye = pr.op("vector", "tensor_scalar", yo[ypar][:],
                                   psY[:], gw[:, e * njt + tc:e * njt + tc + 1],
                                   None, op0=ALU.mult,
                                   waits=[("pe", tok_m2), tok_gws,
                                          tok_yst[ypar]])
                    tok_yrd[ypar] = tok_ye
                    tok_gw_rd = tok_ye
                    tok_yst[ypar] = pr.idma(
                        "Byst" + str(ypar), aps["ys_d"],
                        IndirectOffsetOnAxis(
                            ap=ivx[:, e * njt + tc:e * njt + tc + 1], axis=0),
                        yo[ypar][:], None,
                        bounds_check=rb_tok, oob_is_err=False,
                        waits=[("ve", tok_ye)])
            pr.barrier()

        with contextlib.ExitStack() as st:
            E = st.enter_context
            xa = [E(nc.sbuf_tensor(f"B_xa{i}", [128, D], F32))
                  for i in range(2)]
            yb = [E(nc.sbuf_tensor(f"B_yb{i}", [128, D], F32))
                  for i in range(2)]
            xo = [E(nc.sbuf_tensor(f"B_xo{i}", [128, D], F32))
                  for i in range(2)]
            tok_rd = [0, 0]
            tok_st = [None, None]
            for t in range(nqt):
                b = t % 2
                ta = pr.dma("sync", f"Bxa{b}", xa[b][:],
                            aps["x_in"][t * 128:(t + 1) * 128, :],
                            waits=[("ve", tok_rd[b])])
                tb = pr.dma("sync", f"Byb{b}", yb[b][:],
                            aps["ys_d"][t * 128:(t + 1) * 128, :],
                            waits=[("ve", tok_rd[b])])
                tok_rd[b] = pr.op("vector", "tensor_tensor", xo[b][:],
                                  xa[b][:], yb[b][:], op=ALU.add,
                                  waits=[ta, tb, tok_st[b]])
                tok_st[b] = pr.dma("sync", f"Bxo{b}",
                                   aps["x_out"][t * 128:(t + 1) * 128, :],
                                   xo[b][:], waits=[("ve", tok_rd[b])])
            pr.barrier()


def emit_prep_l1attn(pr, nc, aps, con):
    specs = [
        ("wq", aps["wq"], aps["wqT_d"], True),
        ("wk_a", aps["wk_a"], aps["wkaT_d"], False),
        ("wk_b", aps["wk_b"], aps["wkbT_d"], True),
        ("wv_a", aps["wv_a"], aps["wvaT_d"], False),
        ("wv_b", aps["wv_b"], aps["wvbT_d"], False),
        ("wo", aps["wo"], aps["woT_d"], False),
    ]
    con = dict((k, v) for k, v in con.items() if not k.endswith("_sb"))
    with contextlib.ExitStack() as st:
        E = st.enter_context
        consts_sb(pr, nc, st, aps, con,
                  need=("ident_bf", "ident_f32", "ropeC", "ropeS"))
        wsb = E(nc.sbuf_tensor("P_wsb", [128, 8 * D], BF16))
        pstp = [E(nc.psum_tensor(f"P_ps{i}", [128, D], BF16)) for i in range(2)]
        t1 = E(nc.sbuf_tensor("P_t1", [128, D], F32))
        sw = E(nc.sbuf_tensor("P_sw", [128, D], F32))
        ob = [E(nc.sbuf_tensor(f"P_ob{i}", [128, D], BF16)) for i in range(2)]
        tok_stp = [None, None]
        tok_ev = [0, 0]
        for name, src, dst, fold in specs:
            O, I = src.shape
            na, nch = O // 128, I // 128
            w3 = wsb[:, 0:na * I].rearrange("p (a i) -> p a i", a=na)
            tok_w = pr.dma("gpsimd", "Pw", w3,
                           src.rearrange("(a p) i -> p a i", p=128),
                           waits=[("pe", pr.cnt["pe"]),
                                  ("ve", max(tok_ev)),
                                  ("se", pr.cnt["se"])])
            for c in range(nch):
                b = c % 2

                def tpf(te, c=c, b=b, na=na, w3=w3, tok_w=tok_w,
                        tev=tok_ev[b]):
                    pr.tw(te, tok_w[0], tok_w[1])
                    if tev:
                        pr.tw(te, "ve", tev)
                    last = None
                    for a in range(na):
                        last = te.transpose(
                            pstp[b][:, a * 128:(a + 1) * 128],
                            w3[:, a, c * 128:(c + 1) * 128],
                            con["ident_bf_sb"][:])
                    return last

                tok_tp = pr._push("tensor", tpf, "pe", 1, [])
                if fold:
                    pr.op("vector", "tensor_tensor", t1[:, 0:O],
                          pstp[b][:, 0:O], con["ropeC_sb"][:, 0:O],
                          op=ALU.mult, waits=[("pe", tok_tp)])
                    vps = pstp[b][:, 0:O].rearrange("p (n two) -> p n two",
                                                    two=2)
                    vsw = sw[:, 0:O].rearrange("p (n two) -> p n two", two=2)
                    pr.op("scalar", "copy", vsw[:, :, 0:1], vps[:, :, 1:2],
                          waits=[("pe", tok_tp), ("ve", pr.cnt["ve"])])
                    tok_sc = pr.op("scalar", "copy", vsw[:, :, 1:2],
                                   vps[:, :, 0:1])
                    pr.op("vector", "tensor_tensor", sw[:, 0:O], sw[:, 0:O],
                          con["ropeS_sb"][:, 0:O], op=ALU.mult,
                          waits=[("se", tok_sc)])
                    tok_ev[b] = pr.op("vector", "tensor_tensor", ob[b][:, 0:O],
                                      t1[:, 0:O], sw[:, 0:O], op=ALU.add,
                                      waits=[tok_stp[b]])
                else:
                    tok_ev[b] = pr.op("vector", "tensor_copy", ob[b][:, 0:O],
                                      pstp[b][:, 0:O],
                                      waits=[("pe", tok_tp), tok_stp[b]])
                tok_stp[b] = pr.dma("sync", f"Pst{b}",
                                    dst[c * 128:(c + 1) * 128, 0:O],
                                    ob[b][:, 0:O], waits=[("ve", tok_ev[b])])
        pr.barrier()


def emit_prep_l1ffn(pr, nc, aps, con):
    specs = [
        ("wg", aps["w_gate"], aps["wgT_d"]),
        ("wu", aps["w_up"], aps["wuT_d"]),
        ("wd", aps["w_down"], aps["wdT_d"]),
    ]
    con = dict((k, v) for k, v in con.items() if not k.endswith("_sb"))
    with contextlib.ExitStack() as stc:
        consts_sb(pr, nc, stc, aps, con, need=("ident_bf",))
        for name, src, dst in specs:
            O, I = src.shape
            na, nch = O // 128, I // 128
            with contextlib.ExitStack() as st:
                E = st.enter_context
                wi = [E(nc.sbuf_tensor(f"P_{name}_i{i}", [128, I], I32))
                      for i in range(2)]
                wdq = [E(nc.sbuf_tensor(f"P_{name}_q{i}", [128, I], BF16))
                       for i in range(2)]
                pst = [E(nc.psum_tensor(f"P_{name}_ps{i}", [128, nch * 128],
                                        BF16)) for i in range(2)]
                asm = E(nc.sbuf_tensor(f"P_{name}_T", [128, nch * O], BF16))
                asm3 = asm[:].rearrange("p (c o) -> p c o", c=nch)
                tok_dq = [0, 0]
                tok_ev = [0, 0]
                for a in range(na):
                    b = a % 2
                    tok_i = pr.dma("sync", f"Pwi{b}", wi[b][:],
                                   src[a * 128:(a + 1) * 128, :],
                                   waits=[("ve", tok_dq[b])])
                    tok_dq[b] = pr.op("vector", "tensor_scalar", wdq[b][:],
                                      wi[b][:], DEQ, None, op0=ALU.mult,
                                      waits=[tok_i, ("pe", pr.cnt["pe"])])

                    def tpf(te, a=a, b=b, nch=nch, tdq=tok_dq[b],
                            tev=tok_ev[b]):
                        pr.tw(te, "ve", tdq)
                        if tev:
                            pr.tw(te, "ve", tev)
                        last = None
                        for c in range(nch):
                            last = te.transpose(
                                pst[b][:, c * 128:(c + 1) * 128],
                                wdq[b][:, c * 128:(c + 1) * 128],
                                con["ident_bf_sb"][:])
                        return last

                    tok_tp = pr._push("tensor", tpf, "pe", 1, [])
                    tok_ev[b] = pr.op(
                        "vector", "tensor_copy",
                        asm3[:, :, a * 128:(a + 1) * 128],
                        pst[b][:].rearrange("p (c o) -> p c o", c=nch),
                        waits=[("pe", tok_tp)])
                pr.dma("sync", "PstT", dst.rearrange("(c p) o -> p c o", p=128),
                       asm3, waits=[("ve", max(tok_ev))])
                pr.barrier()


def emit_phase_C(pr, nc, T, aps, con):
    QG = min(512, T)
    nqg = T // QG
    nkt = T // 128
    emit_prep_l1attn(pr, nc, aps, con)
    with contextlib.ExitStack() as stP:
        EP = stP.enter_context
        consts_sb(pr, nc, stP, aps, con,
                  need=("ident_bf", "ident_f32", "ones_f32", "headmask2"))
        v_all = EP(nc.sbuf_tensor("C_vall", [128, nkt * NHEAD * (HD + 1)], BF16))
        bias_all = EP(nc.sbuf_tensor("C_bias", [128, NHEAD], F32))
        v4 = v_all[:].rearrange("p (k h e) -> p k h e", h=NHEAD, e=HD + 1)
        qfm = aps["qfm1_d"]
        kfm = aps["kfm1_d"]

        with contextlib.ExitStack() as st:
            E = st.enter_context
            xlnT = E(nc.sbuf_tensor("C_xlnT", [128, 8 * T], BF16))
            with contextlib.ExitStack() as stln:
                emit_ln_pass(pr, stln, nc, T, aps["x_in"], xlnT, con, "Cln")
            xlnT3 = xlnT[:].rearrange("p (c t) -> p c t", c=8)

            wqT = E(nc.sbuf_tensor("C_wqT", [128, 8 * D], BF16))
            wkaT = E(nc.sbuf_tensor("C_wkaT", [128, 8 * LATD], BF16))
            wkbT = E(nc.sbuf_tensor("C_wkbT", [128, 2 * D], BF16))
            wvaT = E(nc.sbuf_tensor("C_wvaT", [128, 8 * LATD], BF16))
            wvbT = E(nc.sbuf_tensor("C_wvbT", [128, 2 * D], BF16))
            ck = E(nc.sbuf_tensor("C_ck", [128, 2 * T], BF16))
            cv = E(nc.sbuf_tensor("C_cv", [128, 2 * T], BF16))
            qfb = [E(nc.sbuf_tensor(f"C_qf{i}", [128, 512], BF16))
                   for i in range(2)]
            sqb = E(nc.sbuf_tensor("C_sqb", [128, 512], F32))
            t21 = E(nc.sbuf_tensor("C_t21", [2, 1], F32))
            mqq = E(nc.sbuf_tensor("C_mqq", [2, 8], F32))
            mkk = E(nc.sbuf_tensor("C_mkk", [2, 8], F32))
            nbg = E(nc.sbuf_tensor("C_nbg", [2, 8], F32))
            nbgr = E(nc.sbuf_tensor("C_nbgr", [1, NHEAD], F32))
            psq = [E(nc.psum_tensor(f"C_psq{i}", [128, 512], F32))
                   for i in range(2)]
            psv = E(nc.psum_tensor("C_psv", [128, D], F32))
            psN = E(nc.psum_tensor("C_psN", [2, 512], F32))
            psb = E(nc.psum_tensor("C_psb", [128, 1], F32))

            tok_cst = []
            for name, h in (("wqT_d", wqT), ("wkaT_d", wkaT),
                            ("wkbT_d", wkbT), ("wvaT_d", wvaT),
                            ("wvbT_d", wvbT)):
                w, cols = aps[name].shape
                tok_cst.append(pr.dma(
                    "sync", "Cw",
                    h[:, 0:(w // 128) * cols].rearrange(
                        "p (c o) -> p c o", c=w // 128),
                    aps[name].rearrange("(c p) o -> p c o", p=128)))
            pr.op("vector", "memset", mqq[:], 0.0)
            pr.op("vector", "memset", mkk[:], 0.0)
            pr.op("vector", "memset", v4[:, :, :, HD:HD + 1], 1.0)
            tok_w_last = tok_cst[-1]

            wkaT3 = wkaT[:].rearrange("p (c l) -> p c l", c=8)
            wvaT3 = wvaT[:].rearrange("p (c l) -> p c l", c=8)
            for which, wT3, cdst in (("k", wkaT3, ck), ("v", wvaT3, cv)):
                cd3 = cdst[:].rearrange("p (l t) -> p l t", l=2)
                for lc in range(2):
                    for nb in range(T // 512):
                        b = (lc * (T // 512) + nb) % 2

                        def cmm(te, lc=lc, nb=nb, b=b, wT3=wT3,
                                tw=tok_w_last, tev=pr.cnt["ve"]):
                            pr.tw(te, tw[0], tw[1])
                            pr.tw(te, "ve", tev)
                            last = None
                            for c in range(8):
                                last = te.matmul(
                                    psq[b][:],
                                    lhsT=wT3[:, c, lc * 128:(lc + 1) * 128],
                                    rhs=xlnT3[:, c, nb * 512:(nb + 1) * 512],
                                    start=(c == 0), stop=(c == 7))
                            return last

                        tok_c = pr._push("tensor", cmm, "pe", 1, [])
                        pr.op("vector", "tensor_copy",
                              cd3[:, lc, nb * 512:(nb + 1) * 512], psq[b][:],
                              waits=[("pe", tok_c)])
            pr.barrier()

            wqT3 = wqT[:].rearrange("p (c o) -> p c o", c=8)
            wkbT3 = wkbT[:].rearrange("p (l o) -> p l o", l=2)
            ck3 = ck[:].rearrange("p (l t) -> p l t", l=2)
            tok_st = [None, None]
            tok_nprev = 0
            for which in ("q", "k"):
                dstd = qfm if which == "q" else kfm
                macc = mqq if which == "q" else mkk
                for oc in range(8):
                    for nb in range(T // 512):
                        b = (oc * (T // 512) + nb) % 2

                        def pmm(te, oc=oc, nb=nb, b=b, which=which,
                                trd=pr.cnt["ve"], tse=pr.cnt["se"]):
                            pr.tw(te, "ve", trd)
                            pr.tw(te, "se", tse)
                            last = None
                            if which == "q":
                                for c in range(8):
                                    last = te.matmul(
                                        psq[b][:],
                                        lhsT=wqT3[:, c, oc * 128:(oc + 1) * 128],
                                        rhs=xlnT3[:, c, nb * 512:(nb + 1) * 512],
                                        start=(c == 0), stop=(c == 7))
                            else:
                                for lc in range(2):
                                    last = te.matmul(
                                        psq[b][:],
                                        lhsT=wkbT3[:, lc, oc * 128:(oc + 1) * 128],
                                        rhs=ck3[:, lc, nb * 512:(nb + 1) * 512],
                                        start=(lc == 0), stop=(lc == 1))
                            return last

                        tok_p = pr._push("tensor", pmm, "pe", 1, [])
                        tok_ev = pr.op("vector", "tensor_copy", qfb[b][:],
                                       psq[b][:],
                                       waits=[("pe", tok_p), tok_st[b]])
                        tok_st[b] = pr.dma(
                            "sync", f"Cqst{b}",
                            dstd[oc * 128:(oc + 1) * 128,
                                 nb * 512:(nb + 1) * 512],
                            qfb[b][:], waits=[("ve", tok_ev)])
                        tok_sq = pr.op("scalar", "activation", sqb[:],
                                       psq[b][:], AF.Square,
                                       waits=[("ve", tok_ev),
                                              ("pe", tok_nprev)])

                        def nmm(te, tok_sq=tok_sq, tve=pr.cnt["ve"]):
                            pr.tw(te, "se", tok_sq)
                            pr.tw(te, "ve", tve)
                            return te.matmul(psN[:], lhsT=con["headmask2_sb"][:],
                                             rhs=sqb[:], start=True, stop=True)

                        tok_nprev = pr._push("tensor", nmm, "pe", 1, [])
                        pr.op("vector", "tensor_reduce", t21[:], psN[:],
                              axis=AX.X, op=ALU.max,
                              waits=[("pe", tok_nprev)])
                        pr.op("vector", "tensor_tensor", macc[:, oc:oc + 1],
                              macc[:, oc:oc + 1], t21[:], op=ALU.max)
            pr.barrier()

            wvbT3 = wvbT[:].rearrange("p (l o) -> p l o", l=2)
            cv3 = cv[:].rearrange("p (l t) -> p l t", l=2)
            for tc in range(nkt):
                def vmm(te, tc=tc, trd=pr.cnt["ve"]):
                    pr.tw(te, "ve", trd)
                    last = None
                    for nh in range(2):
                        for lc in range(2):
                            last = te.matmul(
                                psv[:, nh * 512:(nh + 1) * 512],
                                lhsT=cv3[:, lc, tc * 128:(tc + 1) * 128],
                                rhs=wvbT3[:, lc, nh * 512:(nh + 1) * 512],
                                start=(lc == 0), stop=(lc == 1))
                    return last

                tok_v = pr._push("tensor", vmm, "pe", 1, [])
                pr.op("vector", "tensor_copy", v4[:, tc, :, 0:HD],
                      psv[:].rearrange("p (h d) -> p h d", h=NHEAD),
                      waits=[("pe", tok_v)])
            pr.op("vector", "tensor_tensor", nbg[:], mqq[:], mkk[:],
                  op=ALU.mult)
            tok_sq2 = pr.op("scalar", "activation", nbg[:], nbg[:], AF.Sqrt,
                            scale=1.0 / 64.0, waits=[("ve", pr.cnt["ve"])])
            tok_nbg = pr.op("vector", "tensor_scalar", nbg[:], nbg[:], 1.0,
                            -1.0, op0=ALU.add, op1=ALU.mult,
                            waits=[("se", tok_sq2)])
            tok_bd = pr.dma("sync", "Cnbg",
                            aps["nbg_d"][0, :].rearrange("(o s) -> s o", s=2),
                            nbg[:], waits=[("ve", tok_nbg)],
                            allow_slow_non_contiguous=True)
            tok_br = pr.dma("sync", "Cnbg", nbgr[:], aps["nbg_d"][0:1, :],
                            waits=[tok_bd])
            for h in range(NHEAD):
                tok_bb = pr.mm(
                    lambda te, h=h: te.matmul(psb[:],
                                              lhsT=con["ones_f32_sb"][:],
                                              rhs=nbgr[:, h:h + 1],
                                              start=True, stop=True),
                    waits=[tok_br, ("se", pr.cnt["se"])])
                pr.op("scalar", "copy", bias_all[:, h:h + 1], psb[:],
                      waits=[("pe", tok_bb)])
            pr.barrier()

        with contextlib.ExitStack() as st:
            E = st.enter_context
            kh2 = [E(nc.sbuf_tensor(f"C_kh{i}", [128, T], BF16))
                   for i in range(2)]
            qh2 = [E(nc.sbuf_tensor(f"C_qh{i}", [128, QG], BF16))
                   for i in range(2)]
            pbuf = [E(nc.sbuf_tensor(f"C_pb{i}", [128, 2 * QG], BF16))
                    for i in range(3)]
            zrec = E(nc.sbuf_tensor("C_zrec", [128, 4], F32))
            obuf = E(nc.sbuf_tensor("C_obuf", [128, 4 * HD], BF16))
            otmp = E(nc.sbuf_tensor("C_otmp", [64, QG], BF16))
            Ofm = E(nc.sbuf_tensor("C_Ofm", [128, 8 * QG], BF16))
            woT = E(nc.sbuf_tensor("C_woT", [128, 8 * D], BF16))
            xr3 = [E(nc.sbuf_tensor(f"C_xr{i}", [128, D], F32))
                   for i in range(2)]
            xn3 = [E(nc.sbuf_tensor(f"C_xn{i}", [128, D], F32))
                   for i in range(2)]
            psS = [E(nc.psum_tensor(f"C_psS{i}", [128, 1024], F32))
                   for i in range(2)]
            psV = [E(nc.psum_tensor(f"C_psV{i}", [128, 1024], BF16))
                   for i in range(4)]
            psVacc = [p[:, 0:2 * (HD + 1)].bitcast(F32) for p in psV]
            psVz = [p[:, 2 * HD:2 * (HD + 1)].bitcast(F32) for p in psV]
            psVo = [p[:, 0:2 * HD].bitcast(F32) for p in psV]
            psVtp = [p[0:64, 512:640] for p in psV]
            Ofm3 = Ofm[:].rearrange("p (c q) -> p c q", c=8)
            woT3 = woT[:].rearrange("p (c o) -> p c o", c=8)

            tok_wo = pr.dma("sync", "Cwo", woT3,
                            aps["woT_d"].rearrange("(c p) o -> p c o", p=128))
            pr.barrier()

            items = [(qg, h) for qg in range(nqg) for h in range(NHEAD)]
            tok_kh_rd = [0, 0]
            tok_hdma = [None] * len(items)

            def fetch_item(i):
                qg, h = items[i]
                g = i % 2
                wk = [("pe", tok_kh_rd[g])]
                t1 = pr.dma("sync", f"Ckh{g}a", kh2[g][0:64, :],
                            kfm[h * HD:(h + 1) * HD, :], waits=wk)
                t2 = pr.dma("sync", f"Ckh{g}b", kh2[g][64:128, :],
                            kfm[h * HD:(h + 1) * HD, :], waits=wk)
                t3 = pr.dma("sync", f"Cqh{g}a", qh2[g][0:64, :],
                            qfm[h * HD:(h + 1) * HD, qg * QG:(qg + 1) * QG],
                            waits=wk)
                t4 = pr.dma("sync", f"Cqh{g}b", qh2[g][64:128, :],
                            qfm[h * HD:(h + 1) * HD, qg * QG:(qg + 1) * QG],
                            waits=wk)
                tok_hdma[i] = (t1, t2, t3, t4)

            tok_bank_rd = [[] for _ in range(2)]
            tok_psV_rd = [[] for _ in range(4)]
            tok_pv_pb = [0] * 3
            tok_exp_pb = [0] * 3
            tok_tp_rd = []
            tok_otmp_dma = None
            tok_ofm_rd = 0
            tok_xn_done = [0, 0]
            tok_xst = [None, None]
            ofm_w = []

            fetch_item(0)
            for i, (qg, h) in enumerate(items):
                g = i % 2
                if i + 1 < len(items):
                    fetch_item(i + 1)
                dmas = tok_hdma[i]
                tok_smm_last = 0
                tok_pv_last = 0
                pend = None

                def emit_pv(pktp):
                    nonlocal tok_pv_last
                    p3 = pktp % 3
                    for kt in (2 * pktp, 2 * pktp + 1):
                        half = (kt % 2) * 512
                        first = (kt == 0)
                        stop_ = (kt == nkt - 1)
                        evac_w = ([w for q in range(4)
                                   for w in tok_psV_rd[q]] if first else [])

                        def pvm(te, kt=kt, p3=p3, half=half, h=h, first=first,
                                stop_=stop_, texp=tok_exp_pb[p3],
                                evac_w=evac_w):
                            pr.tw(te, "se", texp)
                            for s, v in evac_w:
                                pr.tw(te, s, v)
                            last = None
                            for qt in range(4):
                                last = te.matmul(
                                    psVacc[qt],
                                    lhsT=pbuf[p3][:, half + qt * 128:
                                                  half + (qt + 1) * 128],
                                    rhs=v4[:, kt, h, :],
                                    start=first, stop=stop_)
                            return last

                        tok = pr._push("tensor", pvm, "pe", 1, [])
                        tok_pv_pb[p3] = tok
                        tok_pv_last = tok

                for ktp in range(nkt // 2):
                    kt0, kt1 = 2 * ktp, 2 * ktp + 1
                    bi = ktp % 2
                    p3 = ktp % 3

                    def smm(te, kt0=kt0, kt1=kt1, bi=bi, g=g,
                            dmas=dmas, w0=list(tok_bank_rd[bi])):
                        for t in dmas:
                            pr.tw(te, t[0], t[1])
                        for s, v in w0:
                            pr.tw(te, s, v)
                        te.matmul(psS[bi][:, 0:512],
                                  lhsT=kh2[g][0:64, kt0 * 128:(kt0 + 1) * 128],
                                  rhs=qh2[g][0:64, :], start=True, stop=True)
                        return te.matmul(
                            psS[bi][:, 512:1024],
                            lhsT=kh2[g][64:128, kt1 * 128:(kt1 + 1) * 128],
                            rhs=qh2[g][64:128, :], start=True, stop=True)

                    tok_smm = pr._push("tensor", smm, "pe", 1, [])
                    tok_smm_last = tok_smm
                    tok_e = pr.op("scalar", "activation", pbuf[p3][:],
                                  psS[bi][:], AF.Exp, scale=QSC,
                                  bias=bias_all[:, h:h + 1],
                                  waits=[("pe", tok_smm),
                                         ("pe", tok_pv_pb[p3])])
                    tok_exp_pb[p3] = tok_e
                    tok_bank_rd[bi] = [("se", tok_e)]
                    if pend is not None:
                        emit_pv(pend)
                    pend = ktp
                emit_pv(pend)
                tok_kh_rd[g] = tok_smm_last

                for qt in range(4):
                    pr.op("vector", "reciprocal", zrec[:, qt:qt + 1],
                          psVz[qt], waits=[("pe", tok_pv_last)])
                    tok_nm = pr.op("vector", "tensor_scalar",
                                   obuf[:, qt * HD:(qt + 1) * HD],
                                   psVo[qt], zrec[:, qt:qt + 1],
                                   None, op0=ALU.mult)
                    tok_psV_rd[qt] = [("ve", tok_nm)]
                tok_obuf = pr.cnt["ve"]

                def tpm(te, tok_obuf=tok_obuf, wtp=list(tok_tp_rd)):
                    pr.tw(te, "ve", tok_obuf)
                    for s, v in wtp:
                        pr.tw(te, s, v)
                    last = None
                    for qt in range(4):
                        last = te.transpose(
                            psVtp[qt], obuf[:, qt * HD:(qt + 1) * HD],
                            con["ident_bf_sb"][:])
                    return last

                tok_tp = pr._push("tensor", tpm, "pe", 1, [])
                c = h // 2
                tok_tp_rd = []
                for qt in range(4):
                    if h % 2 == 0:
                        tok_cp = pr.op("vector", "tensor_copy",
                                       Ofm3[0:64, c, qt * 128:(qt + 1) * 128],
                                       psVtp[qt],
                                       waits=[("pe", tok_tp),
                                              ("pe", tok_ofm_rd)])
                        ofm_w.append(("ve", tok_cp))
                    else:
                        tok_cp = pr.op("vector", "tensor_copy",
                                       otmp[:, qt * 128:(qt + 1) * 128],
                                       psVtp[qt],
                                       waits=[("pe", tok_tp), tok_otmp_dma])
                    tok_tp_rd.append(("ve", tok_cp))
                    tok_psV_rd[qt].append(("ve", tok_cp))
                if h % 2 == 1:
                    td = pr.dma("sync", "Cob", Ofm3[64:128, c, :], otmp[:],
                                waits=[("ve", pr.cnt["ve"]),
                                       ("pe", tok_ofm_rd)])
                    tok_otmp_dma = td
                    ofm_w.append(td)

                if h != NHEAD - 1:
                    continue
                for tc in range(QG // 128):
                    g2 = tc % 2
                    if g2 == 0:
                        pA_, pB_ = psS[0][:, 0:512], psS[0][:, 512:1024]
                        wbank = list(tok_bank_rd[0])
                    else:
                        pA_, pB_ = psS[1][:, 0:512], psS[1][:, 512:1024]
                        wbank = list(tok_bank_rd[1])
                    tq = qg * (QG // 128) + tc
                    tok_xr = pr.dma("sync", f"Cxr{g2}", xr3[g2][:],
                                    aps["x_in"][tq * 128:(tq + 1) * 128, :],
                                    waits=[("ve", tok_xn_done[g2])])

                    def wmm(te, tc=tc, pA_=pA_, pB_=pB_, wofm=list(ofm_w),
                            wbank=list(wbank), txn=tok_xn_done[tc % 2],
                            tok_wo=tok_wo):
                        pr.tw(te, tok_wo[0], tok_wo[1])
                        for s, v in wofm + wbank:
                            pr.tw(te, s, v)
                        if txn:
                            pr.tw(te, "ve", txn)
                        last = None
                        for cc in range(8):
                            for nh, pp in ((0, pA_), (1, pB_)):
                                last = te.matmul(
                                    pp,
                                    lhsT=Ofm3[:, cc, tc * 128:(tc + 1) * 128],
                                    rhs=woT3[:, cc, nh * 512:(nh + 1) * 512],
                                    start=(cc == 0), stop=(cc == 7))
                        return last

                    tok_w = pr._push("tensor", wmm, "pe", 1, [])
                    tok_ofm_rd = tok_w
                    tok_xa = pr.op("vector", "tensor_tensor",
                                   xn3[g2][:, 0:512], pA_,
                                   xr3[g2][:, 0:512], op=ALU.add,
                                   waits=[("pe", tok_w), tok_xr,
                                          tok_xst[g2]])
                    tok_xb = pr.op("vector", "tensor_tensor",
                                   xn3[g2][:, 512:1024], pB_,
                                   xr3[g2][:, 512:1024], op=ALU.add)
                    tok_xn_done[g2] = tok_xb
                    tok_bank_rd[g2] = [("ve", tok_xa), ("ve", tok_xb)]
                    tok_xst[g2] = pr.dma(
                        "sync", f"Cxst{g2}",
                        aps["x_out"][tq * 128:(tq + 1) * 128, :],
                        xn3[g2][:], waits=[("ve", tok_xb)])
                ofm_w = []
            pr.barrier()


def emit_phase_D(pr, nc, T, aps, con):
    emit_prep_l1ffn(pr, nc, aps, con)
    nhc = FF1 // 128
    with contextlib.ExitStack() as stP:
        EP = stP.enter_context
        consts_sb(pr, nc, stP, aps, con, need=("ident_bf",))
        xlnT = EP(nc.sbuf_tensor("D_xlnT", [128, 8 * T], BF16))
        with contextlib.ExitStack() as stln:
            emit_ln_pass(pr, stln, nc, T, aps["x_in"], xlnT, con, "Dln")
        xlnT3 = xlnT[:].rearrange("p (c t) -> p c t", c=8)

        with contextlib.ExitStack() as st:
            E = st.enter_context
            wdT = E(nc.sbuf_tensor("D_wdT", [128, nhc * D], BF16))
            wgb = [E(nc.sbuf_tensor(f"D_wg{i}", [128, 8 * 512], BF16))
                   for i in range(2)]
            wub = [E(nc.sbuf_tensor(f"D_wu{i}", [128, 8 * 512], BF16))
                   for i in range(2)]
            sg = E(nc.sbuf_tensor("D_sg", [128, 512], F32))
            hT = E(nc.sbuf_tensor("D_hT", [128, nhc * 512], BF16))
            xr = [E(nc.sbuf_tensor(f"D_xr{i}", [128, D], F32)) for i in range(2)]
            yo = [E(nc.sbuf_tensor(f"D_yo{i}", [128, D], F32)) for i in range(2)]
            psG = [E(nc.psum_tensor(f"D_psG{i}", [128, 512], F32))
                   for i in range(2)]
            psU = [E(nc.psum_tensor(f"D_psU{i}", [128, 512], F32))
                   for i in range(2)]
            psY = [E(nc.psum_tensor(f"D_psY{i}", [128, D], F32))
                   for i in range(2)]
            wdT3 = wdT[:].rearrange("p (hc o) -> p hc o", o=D)
            hT3 = hT[:].rearrange("p (hc t) -> p hc t", t=512)

            tok_wd = pr.dma("sync", "DwdT", wdT3,
                            aps["wdT_d"].rearrange("(hc p) o -> p hc o", p=128))
            tok_w_rd = [0, 0]
            tok_h_ve = 0
            tok_mm2 = 0
            tok_sil = [0, 0]
            tok_xr_rd = [0, 0]
            tok_yev = [0, 0]
            tok_yst = [None, None]
            for nb in range(T // 512):
                for hcb in range(nhc // 4):
                    wpar = hcb % 2
                    wg3 = wgb[wpar][:].rearrange("p (c h) -> p c h", c=8)
                    wu3 = wub[wpar][:].rearrange("p (c h) -> p c h", c=8)
                    tok_wg = pr.dma("gpsimd", f"Dwg{wpar}", wg3,
                                    aps["wgT_d"][:, hcb * 512:(hcb + 1) * 512]
                                    .rearrange("(c p) h -> p c h", p=128),
                                    waits=[("pe", tok_w_rd[wpar])])
                    tok_wu = pr.dma("gpsimd", f"Dwu{wpar}", wu3,
                                    aps["wuT_d"][:, hcb * 512:(hcb + 1) * 512]
                                    .rearrange("(c p) h -> p c h", p=128),
                                    waits=[("pe", tok_w_rd[wpar])])
                    for hc4 in range(4):
                        hc = hcb * 4 + hc4
                        par = hc % 2

                        def gm(te, hc4=hc4, nb=nb, par=par, wg3=wg3, wu3=wu3,
                               tok_wg=tok_wg, tok_wu=tok_wu, tsil=tok_sil[par],
                               thve=tok_h_ve):
                            pr.tw(te, tok_wg[0], tok_wg[1])
                            pr.tw(te, tok_wu[0], tok_wu[1])
                            if tsil:
                                pr.tw(te, "se", tsil)
                                pr.tw(te, "ve", thve)
                            last = None
                            for c in range(8):
                                last = te.matmul(
                                    psG[par][:],
                                    lhsT=wg3[:, c, hc4 * 128:(hc4 + 1) * 128],
                                    rhs=xlnT3[:, c, nb * 512:(nb + 1) * 512],
                                    start=(c == 0), stop=(c == 7))
                            for c in range(8):
                                last = te.matmul(
                                    psU[par][:],
                                    lhsT=wu3[:, c, hc4 * 128:(hc4 + 1) * 128],
                                    rhs=xlnT3[:, c, nb * 512:(nb + 1) * 512],
                                    start=(c == 0), stop=(c == 7))
                            return last

                        tok_g = pr._push("tensor", gm, "pe", 1, [])
                        tok_w_rd[wpar] = tok_g
                        tok_si = pr.op("scalar", "activation", sg[:],
                                       psG[par][:], AF.Silu,
                                       waits=[("pe", tok_g),
                                              ("ve", tok_h_ve)])
                        tok_sil[par] = tok_si
                        tok_h_ve = pr.op("vector", "tensor_tensor",
                                         hT3[:, hc, :], sg[:], psU[par][:],
                                         op=ALU.mult,
                                         waits=[("se", tok_si),
                                                ("pe", tok_mm2)])
                for tc in range(4):
                    ypar = tc % 2
                    tq = nb * 4 + tc
                    tok_xr = pr.dma("sync", f"Dxr{ypar}", xr[ypar][:],
                                    aps["x_in"][tq * 128:(tq + 1) * 128, :],
                                    waits=[("ve", tok_xr_rd[ypar])])

                    def m2(te, tc=tc, ypar=ypar, thve=tok_h_ve, twd=tok_wd,
                           tyev=tok_yev[ypar]):
                        pr.tw(te, "ve", thve)
                        pr.tw(te, twd[0], twd[1])
                        if tyev:
                            pr.tw(te, "ve", tyev)
                        last = None
                        for hc in range(nhc):
                            for nh in range(2):
                                last = te.matmul(
                                    psY[ypar][:, nh * 512:(nh + 1) * 512],
                                    lhsT=hT3[:, hc, tc * 128:(tc + 1) * 128],
                                    rhs=wdT3[:, hc, nh * 512:(nh + 1) * 512],
                                    start=(hc == 0), stop=(hc == nhc - 1))
                        return last

                    tok_m2 = pr._push("tensor", m2, "pe", 1, [])
                    tok_mm2 = tok_m2
                    tok_ye = pr.op("vector", "tensor_tensor", yo[ypar][:],
                                   psY[ypar][:], xr[ypar][:], op=ALU.add,
                                   waits=[("pe", tok_m2), tok_xr,
                                          tok_yst[ypar]])
                    tok_xr_rd[ypar] = tok_ye
                    tok_yev[ypar] = tok_ye
                    tok_yst[ypar] = pr.dma(
                        "sync", f"Dyst{ypar}",
                        aps["x_out"][tq * 128:(tq + 1) * 128, :],
                        yo[ypar][:], waits=[("ve", tok_ye)])
            pr.barrier()


PHASE_INPUTS = {
    "A": ["x_in", "WQ", "WK", "WV", "xq_cos", "xq_sin", "xk_cos", "xk_sin",
          "ident_bf", "ident_f32", "ones_f32", "ones_bf"],
    "B": ["x_in", "W1", "b1", "W2", "b2", "Wr", "br", "ident_bf", "ident_f32",
          "ones_bf", "ltri"],
    "C": ["x_in", "wq", "wk_a", "wk_b", "wv_a", "wv_b", "wo", "ropeC", "ropeS",
          "ident_bf", "ident_f32", "ones_f32", "headmask2"],
    "D": ["x_in", "w_gate", "w_up", "w_down", "ident_bf"],
}

INPUT_SHAPES = {
    "WQ": ([D, D], F32), "WK": ([D, D], F32), "WV": ([D, D], F32),
    "W1": ([NE, D, FF0], F32), "b1": ([NE, FF0], F32),
    "W2": ([NE, FF0, D], F32), "b2": ([NE, D], F32),
    "Wr": ([D, NE], F32), "br": ([NE], F32),
    "wq": ([D, D], F32), "wk_a": ([LATD, D], F32), "wk_b": ([D, LATD], F32),
    "wv_a": ([LATD, D], F32), "wv_b": ([D, LATD], F32), "wo": ([D, D], F32),
    "w_gate": ([FF1, D], I32), "w_up": ([FF1, D], I32),
    "w_down": ([D, FF1], I32),
}


def _phase_scratch(nc, phase, T):
    s = {}
    if phase == "A":
        s["qfm_d"] = nc.dram_tensor("qfm_d", [D, T], BF16, kind="Internal").ap()
        s["vtm_d"] = nc.dram_tensor("vtm_d", [T, D], BF16, kind="Internal").ap()
        s["brow_d"] = nc.dram_tensor("brow_d", [1, T], F32, kind="Internal").ap()
    elif phase == "B":
        s["xln_tm_d"] = nc.dram_tensor("xln_tm_d", [T, D], BF16,
                                       kind="Internal").ap()
        s["w_d"] = nc.dram_tensor("w_d", [T, 1], F32, kind="Internal").ap()
        s["inv_d"] = nc.dram_tensor("inv_d", [NE * min(CAP, T), 1], U32,
                                    kind="Internal").ap()
        s["ys_d"] = nc.dram_tensor("ys_d", [T, D], F32, kind="Internal").ap()
    elif phase == "C":
        for n, shp in (("wqT_d", [D, D]), ("wkaT_d", [D, LATD]),
                       ("wkbT_d", [LATD, D]), ("wvaT_d", [D, LATD]),
                       ("wvbT_d", [LATD, D]), ("woT_d", [D, D]),
                       ("qfm1_d", [D, T]), ("kfm1_d", [D, T])):
            s[n] = nc.dram_tensor(n, shp, BF16, kind="Internal").ap()
        s["nbg_d"] = nc.dram_tensor("nbg_d", [1, NHEAD], F32,
                                    kind="Internal").ap()
    elif phase == "D":
        for n, shp in (("wgT_d", [D, FF1]), ("wuT_d", [D, FF1]),
                       ("wdT_d", [FF1, D])):
            s[n] = nc.dram_tensor(n, shp, BF16, kind="Internal").ap()
    return s


EMITTERS = {"A": emit_phase_A, "B": emit_phase_B, "C": emit_phase_C,
            "D": emit_phase_D}


def build_phases_nc(phases, T):
    nc = bass.Bass("TRN2", target_bir_lowering=False)
    con_np = host_consts(T)
    aps = {}
    needed = []
    for ph in phases:
        for n in PHASE_INPUTS[ph]:
            if n != "x_in" and n not in needed:
                needed.append(n)
    for n in needed:
        if n in INPUT_SHAPES:
            shp, dt = INPUT_SHAPES[n]
        else:
            arr = con_np[n]
            shp, dt = list(arr.shape), CONST_DTYPES[n]
        aps[n] = nc.dram_tensor(n, shp, dt, kind="ExternalInput").ap()
    xs = [nc.dram_tensor("x_in", [T, D], F32, kind="ExternalInput").ap()]
    for i in range(len(phases) - 1):
        xs.append(nc.dram_tensor(f"x_mid{i}", [T, D], F32, kind="Internal").ap())
    xs.append(nc.dram_tensor("x_out", [T, D], F32, kind="ExternalOutput").ap())
    for ph in phases:
        aps.update(_phase_scratch(nc, ph, T))

    with contextlib.ExitStack() as st:
        block = st.enter_context(nc.Block())
        pr = Prog(nc, block, st)
        for i, ph in enumerate(phases):
            con = dict(con_np)
            paps = dict(aps)
            paps["x_in"] = xs[i]
            paps["x_out"] = xs[i + 1]
            EMITTERS[ph](pr, nc, T, paps, con)
        pr.finish()
    return nc, needed


def kernel(**inputs):
    X = np.ascontiguousarray(np.asarray(inputs["L0_attn_X"], dtype=np.float32))
    B, T, _ = X.shape
    con = host_consts(T)
    name_map = {
        "WQ": "L0_attn_W_Q", "WK": "L0_attn_W_K", "WV": "L0_attn_W_V",
        "W1": "L0_ffn_W1", "b1": "L0_ffn_b1", "W2": "L0_ffn_W2",
        "b2": "L0_ffn_b2", "Wr": "L0_ffn_Wr", "br": "L0_ffn_br",
        "wq": "L1_attn_wq", "wk_a": "L1_attn_wk_a", "wk_b": "L1_attn_wk_b",
        "wv_a": "L1_attn_wv_a", "wv_b": "L1_attn_wv_b", "wo": "L1_attn_wo",
        "w_gate": "L1_ffn_w_gate", "w_up": "L1_ffn_w_up",
        "w_down": "L1_ffn_w_down",
    }

    def get_input(n):
        if n in name_map:
            arr = np.asarray(inputs[name_map[n]])
            dt = np.int32 if INPUT_SHAPES[n][1] == I32 else np.float32
            return np.ascontiguousarray(arr.astype(dt, copy=False))
        return con[n]

    nc, needed = build_phases_nc(("A", "B", "C", "D"), T)
    base = {n: get_input(n) for n in needed}
    in_maps = [dict(base, x_in=np.ascontiguousarray(X[c])) for c in range(B)]
    res = run_bass_kernel_spmd(nc, in_maps, core_ids=list(range(B)))
    return np.stack([np.asarray(res.results[c]["x_out"], dtype=np.float32)
                     for c in range(B)], axis=0)
